# version 06
# baseline (speedup 1.0000x reference)
import math
from dataclasses import dataclass

import numpy as np

import concourse.bass as bass
import concourse.mybir as mybir
from concourse.bass_utils import run_bass_kernel_spmd

F32 = mybir.dt.float32
BF16 = mybir.dt.bfloat16
U32 = mybir.dt.uint32
AF = mybir.ActivationFunctionType
ALU = mybir.AluOpType
AX = mybir.AxisListType


@dataclass
class Cfg:
    Bc: int = 2
    S: int = 2048
    H: int = 2048
    NH: int = 16
    E: int = 32
    F: int = 768
    Tb: int = 1024
    rope_theta: float = 1e6
    eps: float = 1e-5
    do_ffn1: bool = True
    do_attn: bool = True
    do_ffn2: bool = True
    debug: bool = False

    @property
    def N(self): return self.Bc * self.S
    @property
    def KH(self): return self.H // 128
    @property
    def FT(self): return self.F // 128
    @property
    def IT(self): return self.H // 128
    @property
    def NBLK(self): return self.N // self.Tb
    @property
    def TT(self): return self.Tb // 128
    @property
    def TC(self): return min(512, self.Tb)
    @property
    def NTC(self): return self.Tb // self.TC
    @property
    def QC(self): return min(512, self.S)
    @property
    def NQC(self): return self.S // self.QC
    @property
    def KT(self): return self.S // 128
    @property
    def GH(self): return min(4, self.NH)
    @property
    def NG(self): return self.NH // self.GH
    @property
    def ST(self): return self.S // 128
    @property
    def WDC(self): return min(4, self.IT)
    @property
    def NWD(self): return self.IT // self.WDC


TOPK = 8
SSEL = 4
THRESH = 0.5


class Sem:
    __slots__ = ("h", "n", "name")

    def __init__(self, nc, name):
        self.h = nc.alloc_semaphore(name=name)
        self.n = 0
        self.name = name


class Stream:
    def __init__(self, name):
        self.name = name
        self.ops = []
        self._waited = {}

    def wait(self, sem, val):
        if val <= 0:
            return
        if val > sem.n:
            raise RuntimeError(
                f"{self.name}: wait_ge({sem.name},{val}) exceeds emitted total {sem.n}")
        if self._waited.get(id(sem), -1) >= val:
            return
        self._waited[id(sem)] = val
        self.ops.append(lambda e, h=sem.h, v=val: e.wait_ge(h, v))

    def op(self, meth, *args, inc=None, **kw):
        incs = []
        if inc:
            for sem, amt in inc:
                sem.n += amt
                incs.append((sem.h, amt))

        def f(e, meth=meth, args=args, kw=kw, incs=incs):
            ins = getattr(e, meth)(*args, **kw)
            if incs:
                ins.then_inc(*incs[0])
                for h, amt in incs[1:]:
                    e.nop().then_inc(h, amt)
        self.ops.append(f)


class Prog:
    def __init__(self, nc):
        self.nc = nc
        self.t = Stream("tensor")
        self.v = Stream("vector")
        self.s = Stream("scalar")
        self.g = Stream("gpsimd")
        self.y = Stream("sync")
        self.streams = [self.t, self.v, self.s, self.g, self.y]
        self.bar = Sem(nc, "bar")
        self._sb_cursor = 16512
        self._sb_marks = {}
        self.sems = {}
        self.guards = {}

    def sem(self, name):
        if name not in self.sems:
            self.sems[name] = Sem(self.nc, name)
        return self.sems[name]

    def sb(self, name, shape, dtype, align=64):
        sz = int(np.prod(shape[1:])) * mybir.dt.size(dtype)
        if sz < 64:
            align = 32
        self._sb_cursor = (self._sb_cursor + align - 1) // align * align
        t = self.nc.alloc_sbuf_tensor_at(name, list(shape), dtype, offset=self._sb_cursor)
        self._sb_cursor += sz
        assert self._sb_cursor <= 229344, f"SBUF overflow at {name}: {self._sb_cursor}"
        return t

    def sb_mark(self, tag):
        self._sb_marks[tag] = self._sb_cursor

    def sb_reset(self, tag):
        self._sb_cursor = self._sb_marks[tag]

    def barrier(self):
        targets = [(s, s.n) for s in self.sems.values() if s.n > 0]
        for st in self.streams:
            for s, v in targets:
                st.wait(s, v)

    def guard_set(self, key, *semvals):
        self.guards[key] = list(semvals)

    def guard_wait(self, stream, key):
        for sem, val in self.guards.get(key, ()):
            stream.wait(sem, val)

    def replay(self):
        nc = self.nc
        with nc.Block() as block:
            @block.tensor
            def _(e):
                for f in self.t.ops:
                    f(e)

            @block.vector
            def _(e):
                for f in self.v.ops:
                    f(e)

            @block.scalar
            def _(e):
                for f in self.s.ops:
                    f(e)

            @block.gpsimd
            def _(e):
                for f in self.g.ops:
                    f(e)

            @block.sync
            def _(e):
                for f in self.y.ops:
                    f(e)


def build_nc(cfg: Cfg):
    nc = bass.Bass("TRN2", target_bir_lowering=False, debug=False)
    p = Prog(nc)
    c = cfg

    dx = nc.dram_tensor("x", [c.N, c.H], F32, kind="ExternalInput").ap()
    dwq = nc.dram_tensor("wqt", [c.NH, 128, c.KH * 128], F32, kind="ExternalInput").ap()
    dwk = nc.dram_tensor("wkt", [c.NH, 128, c.KH * 128], F32, kind="ExternalInput").ap()
    dwv = nc.dram_tensor("wvt", [c.NG, 128, c.KH * c.GH * 128], F32, kind="ExternalInput").ap()
    dwo = nc.dram_tensor("wot", [128, c.KH * c.H], F32, kind="ExternalInput").ap()
    dbqc = nc.dram_tensor("bqc", [128, c.KH], F32, kind="ExternalInput").ap()
    dbkc = nc.dram_tensor("bkc", [128, c.KH], F32, kind="ExternalInput").ap()
    dbvr = nc.dram_tensor("bvr", [1, c.H], F32, kind="ExternalInput").ap()
    dgw = nc.dram_tensor("gwt", [128, c.KH, c.E], F32, kind="ExternalInput").ap()
    dwgu = nc.dram_tensor("wgut", [c.E * c.FT, 128, c.KH * 2 * 128], F32,
                          kind="ExternalInput").ap()
    dwd = nc.dram_tensor("wdt", [c.E * c.NWD, 128, c.FT * c.WDC * 128], F32,
                         kind="ExternalInput").ap()
    dsim = nc.dram_tensor("sim", [32, 32], F32, kind="ExternalInput").ap()
    dident = nc.dram_tensor("ident", [128, 128], F32, kind="ExternalInput").ap()
    drt = nc.dram_tensor("rt", [128, 128], F32, kind="ExternalInput").ap()
    dropec = nc.dram_tensor("ropec", [128, c.S], F32, kind="ExternalInput").ap()
    dropes = nc.dram_tensor("ropes", [128, c.S], F32, kind="ExternalInput").ap()
    diota = nc.dram_tensor("iota32", [128, 32], F32, kind="ExternalInput").ap()
    diotam = nc.dram_tensor("iotam", [128, 128], F32, kind="ExternalInput").ap()

    dout = nc.dram_tensor("out", [c.N, c.H], F32, kind="ExternalOutput").ap()
    dx1 = nc.dram_tensor("x1s", [c.N, c.H], F32, kind="Internal").ap()
    dx2 = nc.dram_tensor("x2s", [c.N, c.H], F32, kind="Internal").ap()
    dao = nc.dram_tensor("aos", [c.NQC, 128, c.KH * c.QC], BF16, kind="Internal").ap()
    if c.debug:
        dbg_z = nc.dram_tensor("dbg_z", [c.N, c.H], F32, kind="ExternalOutput").ap()
        dbg_comb = nc.dram_tensor("dbg_comb", [c.NBLK, 32, c.Tb], F32, kind="ExternalOutput").ap()
        dbg_y = nc.dram_tensor("dbg_y", [c.NBLK, 128, c.IT * c.Tb], F32, kind="ExternalOutput").ap()
        dbg_rt = nc.dram_tensor("dbg_rt", [c.N, 80], F32, kind="ExternalOutput").ap()
        sdbg = p.sem("sdbg")

    pb = [nc.alloc_psum_tensor(f"pb{i}", [128, 512], F32) for i in range(8)]

    def pbank(i, ncols=512, off=0):
        return pb[i][:, off:off + ncols]

    def pbank_bf(i, ncols, off=0):
        return pb[i].ap().bitcast(BF16)[:, off:off + ncols]

    ident32 = p.sb("ident32", [128, 128], F32)
    identbf = p.sb("identbf", [128, 128], BF16)
    rtbf = p.sb("rtbf", [128, 128], BF16)
    iota32 = p.sb("iota32", [128, 32], F32)
    iotam = p.sb("iotam", [128, 128], F32)
    gwt_sb = p.sb("gwt_sb", [128, c.KH, c.E], F32)
    simbd = p.sb("simbd", [128, 128], F32)
    bqc_sb = p.sb("bqc_sb", [128, c.KH], F32)
    bkc_sb = p.sb("bkc_sb", [128, c.KH], F32)
    bvr_sb = p.sb("bvr_sb", [1, c.H], BF16)
    ones_col_bf = p.sb("ones_col_bf", [128, 1], BF16)
    ones_row_bf = p.sb("ones_row_bf", [1, 128], BF16)
    ones_row_f = p.sb("ones_row_f", [1, 128], F32)
    p.sb_mark("phase")

    sd = p.sem("sd")
    sg0 = p.sem("sg0")
    sv0 = p.sem("sv0")

    p.y.op("dma_start", out=ident32[:, :], in_=dident, inc=[(sd, 16)])
    p.y.op("dma_start", out=iota32[:, :], in_=diota, inc=[(sd, 16)])
    p.y.op("dma_start", out=iotam[:, :], in_=diotam, inc=[(sd, 16)])
    p.y.op("dma_start", out=gwt_sb[:, :, :], in_=dgw, inc=[(sd, 16)])
    p.y.op("dma_start", out=bqc_sb[:, :], in_=dbqc, inc=[(sd, 16)])
    p.y.op("dma_start", out=bkc_sb[:, :], in_=dbkc, inc=[(sd, 16)])
    p.g.op("dma_start", out=identbf[:, :], in_=dident, inc=[(sg0, 16)])
    p.g.op("dma_start", out=rtbf[:, :], in_=drt, inc=[(sg0, 16)])
    p.g.op("dma_start", out=bvr_sb[:, :], in_=dbvr, inc=[(sg0, 16)])
    p.v.op("memset", simbd[:, :], 0.0, inc=[(sv0, 1)])
    p.y.wait(sv0, 1)
    for b in range(4):
        p.y.op("dma_start", out=simbd[32 * b:32 * (b + 1), 32 * b:32 * (b + 1)],
               in_=dsim, inc=[(sd, 16)])
    p.v.op("memset", ones_col_bf[:, :], 1.0)
    p.v.op("memset", ones_row_bf[:, :], 1.0)
    p.v.op("memset", ones_row_f[:, :], 1.0)
    p.y.wait(sd, sd.n)
    p.g.wait(sg0, sg0.n)
    p.barrier()

    def emit_ffn(tag, src, dst):
        p.sb_reset("phase")
        _zT0 = p.sb(f"{tag}_zT0", [128, c.KH, c.Tb], BF16)
        zT = [_zT0, _zT0]
        yT = p.sb(f"{tag}_y", [128, c.TT, c.H], F32)
        wgu = [p.sb(f"{tag}_wgu{i}", [128, c.KH, 2, 128], BF16) for i in range(2)]
        wd = [p.sb(f"{tag}_wd{i}", [128, c.FT, c.WDC * 128], BF16) for i in range(2)]
        hsb = [p.sb(f"{tag}_h{i}", [128, c.FT, c.TC], BF16) for i in range(2)]
        xr = [p.sb(f"{tag}_xr{i}", [128, c.H], F32) for i in range(2)]
        xo = [p.sb(f"{tag}_xo{i}", [128, c.H], F32) for i in range(2)]
        _z320 = p.sb(f"{tag}_z320", [128, c.H], F32)
        z32 = [_z320, _z320]
        _zbf0 = p.sb(f"{tag}_zbf0", [128, c.H], BF16)
        zbf = [_zbf0, _zbf0]
        sq = xo[0]
        zfT = [p.sb(f"{tag}_zfT{i}", [128, 128], F32) for i in range(2)]
        comb_blk = p.sb(f"{tag}_combblk", [128, c.TT, 32], F32)
        silu_sb = p.sb(f"{tag}_silu", [128, c.TC], F32)
        lgs = p.sb(f"{tag}_lg", [128, 32], F32)
        topv = p.sb(f"{tag}_topv", [128, 8], F32)
        topi = p.sb(f"{tag}_topi", [128, 8], U32)
        topif = p.sb(f"{tag}_topif", [128, 8], F32)
        nl0 = p.sb(f"{tag}_nl0", [128, 1], F32)
        expw = p.sb(f"{tag}_expw", [128, 8], F32)
        sumw = p.sb(f"{tag}_sumw", [128, 1], F32)
        rcpw = p.sb(f"{tag}_rcpw", [128, 1], F32)
        rw = p.sb(f"{tag}_rw", [128, 8], F32)
        ohsec = p.sb(f"{tag}_ohsec", [128, 128], F32)
        ohsecT = p.sb(f"{tag}_ohsecT", [128, 128], F32)
        simrows = p.sb(f"{tag}_simrows", [128, 128], F32)
        ohrep = p.sb(f"{tag}_ohrep", [128, 128], F32)
        prod = ohsecT
        sc = p.sb(f"{tag}_sc", [128, 16], F32)
        bs = p.sb(f"{tag}_bs", [128, 4], F32)
        mx8 = simrows.ap().bitcast(F32)[:, 0:8]
        scpad = simrows[:, 8:16]
        ix8 = simrows.ap().bitcast(U32)[:, 16:24]
        bif = p.sb(f"{tag}_bif", [128, 4], F32)
        o4 = simrows[:, 24:28]
        t4 = simrows[:, 28:32]
        bp = simrows[:, 36:40]
        msk = simrows[:, 32:36]
        ns = simrows[:, 40:44]
        comb = simrows[:, 44:76]
        mstat = p.sb(f"{tag}_mstat", [128, 1], F32)
        ssq = p.sb(f"{tag}_ssq", [128, 1], F32)
        v1e = p.sb(f"{tag}_v1e", [128, 1], F32)
        mm2 = p.sb(f"{tag}_mm2", [128, 1], F32)
        varv = p.sb(f"{tag}_varv", [128, 1], F32)
        rcpv = p.sb(f"{tag}_rcpv", [128, 1], F32)
        rstd = p.sb(f"{tag}_rstd", [128, 1], F32)
        nmr = p.sb(f"{tag}_nmr", [128, 1], F32)

        sx = [p.sem(f"{tag}_sxA"), p.sem(f"{tag}_sxB")]
        sxo = [p.sem(f"{tag}_sxoA"), p.sem(f"{tag}_sxoB")]
        swg = [p.sem(f"{tag}_swgA"), p.sem(f"{tag}_swgB")]
        swd = [p.sem(f"{tag}_swdA"), p.sem(f"{tag}_swdB")]
        sv = p.sem(f"{tag}_sv")
        ss = p.sem(f"{tag}_ss")
        st = p.sem(f"{tag}_st")
        shp = p.sem(f"{tag}_shp")
        sgu = p.sem(f"{tag}_sgu")
        syp = p.sem(f"{tag}_syp")
        sye = p.sem(f"{tag}_sye")
        srt = p.sem(f"{tag}_srt")
        sdr = p.sem(f"{tag}_sdr")
        vself = p.sem(f"{tag}_vself")

        def vh(meth, *args, extra=None, **kw):
            inc = [(sv, 1)] + (extra or [])
            p.v.op(meth, *args, inc=inc, **kw)
            p.v.wait(sv, sv.n)

        def ln_router(blk, tt):
            par = blk % 2
            slot = tt % 2
            t0 = blk * c.Tb + tt * 128
            p.guard_wait(p.y, ("fx", slot))
            p.y.op("dma_start", out=xr[slot][:, :], in_=src[t0:t0 + 128, :],
                   inc=[(sx[slot], 16)])
            xa = xr[slot]
            p.v.wait(sx[slot], sx[slot].n)
            vh("reduce_sum", mstat[:, :], xa[:, :], axis=AX.X)
            p.s.wait(sx[slot], sx[slot].n)
            p.s.wait(sxo[0], sxo[0].n)
            p.s.op("activation", sq[:, :], xa[:, :], AF.Square, accum_out=ssq[:, :],
                   inc=[(ss, 1)])
            p.v.wait(ss, ss.n)
            vh("tensor_scalar", v1e[:, :], ssq[:, :], 1.0 / c.H, c.eps, ALU.mult, ALU.add)
            vh("tensor_scalar", mstat[:, :], mstat[:, :], 1.0 / c.H, None, ALU.mult)
            vh("tensor_mul", mm2[:, :], mstat[:, :], mstat[:, :])
            vh("tensor_sub", varv[:, :], v1e[:, :], mm2[:, :])
            vh("reciprocal", rcpv[:, :], varv[:, :])
            p.s.wait(sv, sv.n)
            p.s.op("activation", rstd[:, :], rcpv[:, :], AF.Sqrt, inc=[(ss, 1)])
            p.v.wait(ss, ss.n)
            vh("tensor_scalar", nmr[:, :], mstat[:, :], rstd[:, :], -1.0, ALU.mult, ALU.mult)
            p.guard_wait(p.v, ("z32",))
            vh("tensor_scalar", z32[slot][:, :], xa[:, :], rstd[:, :], nmr[:, :],
               ALU.mult, ALU.add)
            p.guard_set(("fx", slot), (sv, sv.n), (ss, ss.n))
            vh("tensor_copy", zbf[slot][:, :], z32[slot][:, :])
            if c.debug and tag == "f1":
                p.y.wait(sv, sv.n)
                p.y.op("dma_start", out=dbg_z[t0:t0 + 128, :], in_=z32[slot][:, :],
                       inc=[(p.sem("sdbg"), 16)])

            p.t.wait(sv, sv.n)
            lgp = pbank(7, c.E, 0)
            for k in range(c.KH):
                dst_bf = pbank_bf(k % 2, 128, 0)
                if k >= 2:
                    p.t.wait(sv, sv.n)
                p.t.op("transpose", dst_bf, zbf[slot][:, k * 128:(k + 1) * 128],
                       identbf[:, :], inc=[(st, 1)])
                p.v.wait(st, st.n)
                p.v.op("tensor_copy", zT[par][:, k, tt * 128:(tt + 1) * 128], dst_bf,
                       inc=[(sv, 1)])
                dst_f = pbank(2 + k % 2, 128, 0)
                if k >= 2:
                    p.t.wait(ss, ss.n)
                p.t.op("transpose", dst_f, z32[slot][:, k * 128:(k + 1) * 128],
                       ident32[:, :], inc=[(st, 1)])
                p.s.wait(st, st.n)
                p.s.op("copy", zfT[k % 2][:, :], dst_f, inc=[(ss, 1)])
                p.t.wait(ss, ss.n)
                p.t.op("matmul", lgp, zfT[k % 2][:, :], gwt_sb[:, k, :c.E],
                       start=(k == 0), stop=(k == c.KH - 1),
                       inc=[(st, 1)] if k == c.KH - 1 else None)

            p.guard_set(("z32",), (st, st.n))
            p.v.wait(st, st.n)
            if c.E < 32:
                vh("memset", lgs[:, c.E:], -1e30)
            vh("tensor_copy", lgs[:, :c.E], lgp)
            vh("max", out=topv[:, :], in_=lgs[:, :])
            vh("max_index", out=topi[:, :], in_max=topv[:, :], in_values=lgs[:, :])
            vh("tensor_copy", topif[:, :], topi[:, :])
            vh("tensor_scalar", nl0[:, :], topv[:, 0:1], -1.0, None, ALU.mult)
            p.s.wait(sv, sv.n)
            p.s.op("activation", expw[:, :], topv[:, :], AF.Exp, bias=nl0[:, :],
                   inc=[(ss, 1)])
            p.v.wait(ss, ss.n)
            vh("reduce_sum", sumw[:, :], expw[:, :], axis=AX.X)
            vh("reciprocal", rcpw[:, :], sumw[:, :])
            vh("tensor_scalar", rw[:, :], expw[:, :], rcpw[:, :], 0.5, ALU.mult, ALU.mult)
            for j in range(4):
                vh("tensor_scalar", ohsec[:, 32 * j:32 * (j + 1)], iota32[:, :],
                   topif[:, 4 + j:5 + j], None, ALU.is_equal)
            p.t.wait(sv, sv.n)
            ohp = pbank(4, 128, 0)
            p.t.op("transpose", ohp, ohsec[:, :], ident32[:, :], inc=[(st, 1)])
            p.s.wait(st, st.n)
            p.s.op("copy", ohsecT[:, :], ohp, inc=[(ss, 1)])
            p.t.wait(ss, ss.n)
            srp = pbank(5, 128, 0)
            p.t.op("matmul", srp, ohsecT[:, :], simbd[:, :], start=True, stop=True,
                   inc=[(st, 1)])
            p.v.wait(st, st.n)
            vh("tensor_copy", simrows[:, :], srp)
            for i in range(4):
                vh("tensor_scalar", ohrep[:, :], iotam[:, :], topif[:, i:i + 1],
                   None, ALU.is_equal)
                vh("tensor_mul", prod[:, :], simrows[:, :], ohrep[:, :])
                vh("tensor_reduce",
                   sc.ap().rearrange("p (j i) -> p j i", i=4)[:, :, i:i + 1],
                   prod.ap().rearrange("p (j e) -> p j e", e=32), AX.X, ALU.add)
            vh("memset", scpad[:, :], -1e30)
            for j in range(4):
                vh("tensor_copy", scpad[:, 0:4], sc[:, 4 * j:4 * (j + 1)])
                vh("max", out=mx8[:, :], in_=scpad[:, :])
                vh("max_index", out=ix8[:, :], in_max=mx8[:, :], in_values=scpad[:, :])
                vh("tensor_copy", bs[:, j:j + 1], mx8[:, 0:1])
                vh("tensor_copy", bif[:, j:j + 1], ix8[:, 0:1])
            for j in range(4):
                vh("tensor_scalar", o4[:, :], iota32[:, 0:4], bif[:, j:j + 1],
                   None, ALU.is_equal)
                vh("tensor_mul", t4[:, :], o4[:, :], topif[:, 0:4])
                vh("reduce_sum", bp[:, j:j + 1], t4[:, :], axis=AX.X)
            vh("tensor_scalar", msk[:, :], bs[:, :], THRESH, None, ALU.is_ge)
            vh("tensor_sub", t4[:, :], bp[:, :], topif[:, 4:8])
            vh("tensor_mul", t4[:, :], msk[:, :], t4[:, :])
            vh("tensor_add", ns[:, :], topif[:, 4:8], t4[:, :])
            for j in range(8):
                idx = topif[:, j:j + 1] if j < 4 else ns[:, j - 4:j - 3]
                vh("tensor_scalar", ohrep[:, 0:32], iota32[:, :], idx, None,
                   ALU.is_equal)
                if j == 0:
                    vh("tensor_scalar", comb[:, :], ohrep[:, 0:32], rw[:, 0:1],
                       None, ALU.mult)
                else:
                    vh("scalar_tensor_tensor", comb[:, :], ohrep[:, 0:32],
                       rw[:, j:j + 1], comb[:, :], ALU.mult, ALU.add)
            vh("tensor_copy", comb_blk[:, tt, :], comb[:, :], extra=[(srt, 1)])
            if c.debug and tag == "f1":
                dbgv = p.sb(f"{tag}_dbgrt_{blk}_{tt}", [128, 80], F32)
                p.v.op("tensor_copy", dbgv[:, 0:32], lgs[:, :])
                p.v.op("tensor_copy", dbgv[:, 32:40], topv[:, :])
                p.v.op("tensor_copy", dbgv[:, 40:48], topif[:, :])
                p.v.op("tensor_copy", dbgv[:, 48:56], rw[:, :])
                p.v.op("tensor_copy", dbgv[:, 56:60], bs[:, :])
                p.v.op("tensor_copy", dbgv[:, 60:64], bif[:, :])
                p.v.op("tensor_copy", dbgv[:, 64:68], bp[:, :])
                p.v.op("tensor_copy", dbgv[:, 68:72], ns[:, :])
                p.v.op("tensor_copy", dbgv[:, 72:76], sc[:, 0:4])
                p.v.op("tensor_copy", dbgv[:, 76:80], sc[:, 4:8], inc=[(p.sem("sdv"), 1)])
                p.y.wait(p.sem("sdv"), p.sem("sdv").n)
                p.y.op("dma_start", out=dbg_rt[t0:t0 + 128, :], in_=dbgv[:, :],
                       inc=[(p.sem("sdbg"), 16)])

        pair_ctr = [0]
        silu_val = {}
        setfree = {}
        nchunks_gu = c.NBLK * c.E * c.FT
        nchunks_wd = c.NBLK * c.E * c.NWD
        wgu_free = {}
        wd_free = {}
        srt_block = {}

        def issue_wgu(gc):
            if gc >= nchunks_gu:
                return
            e = (gc // c.FT) % c.E
            j = gc % c.FT
            slot = gc % 2
            if gc - 2 in wgu_free:
                p.g.wait(ss, wgu_free[gc - 2])
            p.g.op("dma_start",
                   out=wgu[slot].ap().rearrange("p k s f -> p (k s f)"),
                   in_=dwgu[e * c.FT + j], inc=[(swg[slot], 16)])

        def issue_wd(gd):
            if gd >= nchunks_wd:
                return
            e = (gd // c.NWD) % c.E
            gch = gd % c.NWD
            slot = gd % 2
            if gd - 2 in wd_free:
                p.g.wait(sye, wd_free[gd - 2])
            p.g.op("dma_start",
                   out=wd[slot].ap().rearrange("p a b -> p (a b)"),
                   in_=dwd[e * c.NWD + gch], inc=[(swd[slot], 16)])

        def acquire_set(ctr, width):
            for sem, val in setfree.get(ctr - 2, ()):
                p.t.wait(sem, val)
            b = 0 if ctr % 2 == 0 else 4
            return [pbank(b + i, width) for i in range(4)]

        def experts(blk):
            par = blk % 2
            p.t.wait(sv, sv.n)
            p.t.wait(ss, ss.n)
            srt_block[blk] = srt.n
            for e in range(c.E):
                for j in range(c.FT):
                    gc = (blk * c.E + e) * c.FT + j
                    wslot = gc % 2
                    p.t.wait(swg[wslot], swg[wslot].n)
                    for tc in range(c.NTC):
                        pi = pair_ctr[0]
                        if pi - 2 in silu_val:
                            p.t.wait(ss, silu_val[pi - 2])
                        banks = acquire_set(pi, c.TC)
                        gp, up = banks[0], banks[1]
                        for k in range(c.KH):
                            last = (k == c.KH - 1)
                            p.t.op("matmul", gp, wgu[wslot][:, k, 0, :],
                                   zT[par][:, k, tc * c.TC:(tc + 1) * c.TC],
                                   start=(k == 0), stop=last)
                            p.t.op("matmul", up, wgu[wslot][:, k, 1, :],
                                   zT[par][:, k, tc * c.TC:(tc + 1) * c.TC],
                                   start=(k == 0), stop=last,
                                   inc=[(sgu, 1)] if last else None)
                        hslot = tc % 2
                        p.s.wait(sgu, sgu.n)
                        p.s.wait(shp, shp.n)
                        p.s.op("activation", silu_sb[:, :], gp, AF.Silu,
                               inc=[(ss, 1)])
                        p.v.wait(ss, ss.n)
                        p.guard_wait(p.v, ("h", hslot))
                        p.v.op("tensor_tensor", hsb[hslot][:, j, :], silu_sb[:, :],
                               up, ALU.mult, inc=[(shp, 1)])
                        silu_val[pi] = ss.n
                        setfree[pi] = [(ss, ss.n), (shp, shp.n)]
                        pair_ctr[0] += 1
                    wgu_free[gc] = ss.n
                    issue_wgu(gc + 2)

                p.v.wait(srt, srt_block[blk])
                for gch in range(c.NWD):
                    gd = (blk * c.E + e) * c.NWD + gch
                    wslot = gd % 2
                    p.t.wait(swd[wslot], swd[wslot].n)
                    for tc in range(c.NTC):
                        hslot = tc % 2
                        p.t.wait(shp, shp.n - (c.NTC - 1 - tc))
                        ICW = min(512, c.WDC * 128)
                        NIC = (c.WDC * 128) // ICW
                        for tq in range(c.TC // 128):
                            ttile = tc * (c.TC // 128) + tq
                            ctr = pair_ctr[0]
                            ybanks = acquire_set(ctr, ICW)
                            ic0 = gch * c.WDC * 128
                            for ft in range(c.FT):
                                for icq in range(NIC):
                                    p.t.op("matmul", ybanks[icq],
                                           hsb[hslot][:, ft, tq * 128:(tq + 1) * 128],
                                           wd[wslot][:, ft, icq * ICW:(icq + 1) * ICW],
                                           start=(ft == 0), stop=(ft == c.FT - 1),
                                           inc=[(syp, 1)] if ft == c.FT - 1 else None)
                            p.v.wait(syp, syp.n)
                            cw = comb_blk[:, ttile, e:e + 1]
                            for icq in range(NIC):
                                ycol = yT[:, ttile,
                                          ic0 + icq * ICW:ic0 + (icq + 1) * ICW]
                                if e == 0:
                                    p.v.op("tensor_scalar", ycol, ybanks[icq],
                                           cw, None, ALU.mult, inc=[(sye, 1)])
                                else:
                                    p.v.op("scalar_tensor_tensor", ycol,
                                           ybanks[icq], cw, ycol,
                                           ALU.mult, ALU.add, inc=[(sye, 1)])
                            if gch == c.NWD - 1 and tq == c.TC // 128 - 1:
                                p.guard_set(("h", hslot), (sye, sye.n))
                            setfree[ctr] = [(sye, sye.n)]
                            pair_ctr[0] += 1
                    wd_free[gd] = sye.n
                    issue_wd(gd + 2)

        def drain(blk):
            p.v.wait(ss, ss.n)
            for tt in range(c.TT):
                t0 = blk * c.Tb + tt * 128
                slot = tt % 2
                oslot = tt % 2
                p.guard_wait(p.y, ("fx", slot))
                p.y.op("dma_start", out=xr[slot][:, :], in_=src[t0:t0 + 128, :],
                       inc=[(sx[slot], 16)])
                p.v.wait(sxo[oslot], sxo[oslot].n)
                p.v.wait(sx[slot], sx[slot].n)
                p.v.op("tensor_tensor", xo[oslot][:, :], xr[slot][:, :],
                       yT[:, tt, :], ALU.add, inc=[(sdr, 1)])
                p.guard_set(("fx", slot), (sdr, sdr.n))
                p.y.wait(sdr, sdr.n)
                p.y.op("dma_start", out=dst[t0:t0 + 128, :], in_=xo[oslot][:, :],
                       inc=[(sxo[oslot], 16)])

        for gci in range(2):
            issue_wgu(gci)
            issue_wd(gci)
        for blk in range(c.NBLK):
            for tt in range(c.TT):
                ln_router(blk, tt)
            experts(blk)
            drain(blk)
        p.y.wait(sxo[0], sxo[0].n)
        p.y.wait(sxo[1], sxo[1].n)
        p.barrier()

    def emit_attn(tag, src, dst):
        p.sb_reset("phase")
        GW = c.GH * 128
        xr = [p.sb(f"{tag}_xr{i}", [128, c.H], F32) for i in range(2)]
        ropec_sb = p.sb(f"{tag}_ropec", [128, c.S], BF16)
        ropes_sb = p.sb(f"{tag}_ropes", [128, c.S], BF16)
        p.sb_mark("attn_oproj")
        zTa = p.sb(f"{tag}_zT", [128, c.KH, c.S], BF16)
        vTg = p.sb(f"{tag}_vTg", [128, c.ST, GW], BF16)
        qr = [p.sb(f"{tag}_qr{i}", [128, c.S], BF16) for i in range(2)]
        kr = [p.sb(f"{tag}_kr{i}", [128, c.S], BF16) for i in range(2)]
        wsq = [p.sb(f"{tag}_wsq{i}", [128, c.KH, 128], BF16) for i in range(2)]
        wsk = [p.sb(f"{tag}_wsk{i}", [128, c.KH, 128], BF16) for i in range(2)]
        wvsl = p.sb(f"{tag}_wvsl", [128, c.KH, GW], BF16)
        expT = [p.sb(f"{tag}_expT{i}", [128, c.KT, c.QC], BF16) for i in range(2)]
        tmp1 = p.sb(f"{tag}_tmp1", [128, 512], BF16)
        tmp2 = p.sb(f"{tag}_tmp2", [128, 512], BF16)
        sums_sb = p.sb(f"{tag}_sums", [1, c.QC], F32)
        rsum = p.sb(f"{tag}_rsum", [1, c.QC], F32)
        bc_sb = p.sb(f"{tag}_bc", [128, c.QC], F32)
        ao_sb = [p.sb(f"{tag}_ao{i}", [128, c.QC], BF16) for i in range(2)]
        _azbf = p.sb(f"{tag}_zbf0", [128, c.H], BF16)
        zbf = [_azbf, _azbf]
        sq = expT[0].ap().rearrange("p k q -> p (k q)").bitcast(F32)[:, 0:c.H]
        mstat = p.sb(f"{tag}_mstat", [128, 1], F32)
        ssq = p.sb(f"{tag}_ssq", [128, 1], F32)
        v1e = p.sb(f"{tag}_v1e", [128, 1], F32)
        mm2 = p.sb(f"{tag}_mm2", [128, 1], F32)
        varv = p.sb(f"{tag}_varv", [128, 1], F32)
        rcpv = p.sb(f"{tag}_rcpv", [128, 1], F32)
        rstd = p.sb(f"{tag}_rstd", [128, 1], F32)
        nmr = p.sb(f"{tag}_nmr", [128, 1], F32)

        sx = [p.sem(f"{tag}_sxA"), p.sem(f"{tag}_sxB")]
        sxo = [p.sem(f"{tag}_sxoA"), p.sem(f"{tag}_sxoB")]
        sv = p.sem(f"{tag}_sv")
        ss = p.sem(f"{tag}_ss")
        st = p.sem(f"{tag}_st")
        swv = p.sem(f"{tag}_swv")
        swq = [p.sem(f"{tag}_swqA"), p.sem(f"{tag}_swqB")]
        swk = [p.sem(f"{tag}_swkA"), p.sem(f"{tag}_swkB")]
        svt = p.sem(f"{tag}_svt")
        sqp = p.sem(f"{tag}_sqp")
        sex = p.sem(f"{tag}_sex")
        sbc = p.sem(f"{tag}_sbc")
        sao = [p.sem(f"{tag}_saoA"), p.sem(f"{tag}_saoB")]
        saov = p.sem(f"{tag}_saov")
        sai = p.sem(f"{tag}_sai")
        swo = p.sem(f"{tag}_swo")
        sop = p.sem(f"{tag}_sop")
        sdr = p.sem(f"{tag}_sdr")

        def vh(meth, *args, extra=None, **kw):
            inc = [(sv, 1)] + (extra or [])
            p.v.op(meth, *args, inc=inc, **kw)
            p.v.wait(sv, sv.n)

        p.g.op("dma_start", out=ropec_sb[:, :], in_=dropec, inc=[(sg0, 16)])
        p.g.op("dma_start", out=ropes_sb[:, :], in_=dropes, inc=[(sg0, 16)])
        p.v.wait(sg0, sg0.n)
        p.t.wait(sg0, sg0.n)

        def emit_ln(seq):
            base = seq * c.S
            for tt in range(c.ST):
                t0 = base + tt * 128
                slot = tt % 2
                p.guard_wait(p.y, ("ax", slot))
                p.y.op("dma_start", out=xr[slot][:, :], in_=src[t0:t0 + 128, :],
                       inc=[(sx[slot], 16)])
                xa = xr[slot]
                p.v.wait(sx[slot], sx[slot].n)
                vh("reduce_sum", mstat[:, :], xa[:, :], axis=AX.X)
                p.s.wait(sx[slot], sx[slot].n)
                p.s.op("activation", sq[:, :], xa[:, :], AF.Square, accum_out=ssq[:, :],
                       inc=[(ss, 1)])
                p.v.wait(ss, ss.n)
                vh("tensor_scalar", v1e[:, :], ssq[:, :], 1.0 / c.H, c.eps, ALU.mult, ALU.add)
                vh("tensor_scalar", mstat[:, :], mstat[:, :], 1.0 / c.H, None, ALU.mult)
                vh("tensor_mul", mm2[:, :], mstat[:, :], mstat[:, :])
                vh("tensor_sub", varv[:, :], v1e[:, :], mm2[:, :])
                vh("reciprocal", rcpv[:, :], varv[:, :])
                p.s.wait(sv, sv.n)
                p.s.op("activation", rstd[:, :], rcpv[:, :], AF.Sqrt, inc=[(ss, 1)])
                p.v.wait(ss, ss.n)
                vh("tensor_scalar", nmr[:, :], mstat[:, :], rstd[:, :], -1.0, ALU.mult, ALU.mult)
                p.guard_wait(p.v, ("azbf",))
                vh("tensor_scalar", zbf[slot][:, :], xa[:, :], rstd[:, :], nmr[:, :],
                   ALU.mult, ALU.add)
                p.guard_set(("ax", slot), (sv, sv.n), (ss, ss.n))
                p.t.wait(sv, sv.n)
                for k in range(c.KH):
                    dst_bf = pbank_bf(k % 2, 128, 0)
                    if k >= 2:
                        p.t.wait(sv, sv.n)
                    p.t.op("transpose", dst_bf, zbf[slot][:, k * 128:(k + 1) * 128],
                           identbf[:, :], inc=[(st, 1)])
                    p.v.wait(st, st.n)
                    p.v.op("tensor_copy", zTa[:, k, tt * 128:(tt + 1) * 128], dst_bf,
                           inc=[(sv, 1)])
                p.guard_set(("azbf",), (st, st.n))

        proj_ctr = [0]
        ao_ctr = [0]
        wsk_dummy = None

        def issue_slices(hd):
            qs = hd % 2
            p.g.wait(sqp, sqp.n - 2 * c.NQC)
            p.g.op("dma_start",
                   out=wsq[qs].ap().rearrange("p a b -> p (a b)"),
                   in_=dwq[hd], inc=[(swq[qs], 16)])
            p.g.op("dma_start",
                   out=wsk[qs].ap().rearrange("p a b -> p (a b)"),
                   in_=dwk[hd], inc=[(swk[qs], 16)])

        def emit_heads(seq):
            for g in range(c.NG):
                p.g.wait(svt, svt.n)
                p.g.wait(st, st.n)
                p.g.op("dma_start",
                       out=wvsl.ap().rearrange("p a b -> p (a b)"),
                       in_=dwv[g], inc=[(swv, 16)])
                p.t.wait(swv, swv.n)
                p.t.wait(sv, sv.n)
                p.t.wait(sqp, sqp.n)
                p.t.wait(sex, sex.n)
                for r in range(c.ST):
                    vp = pbank(r % 2, GW)
                    if r >= 2:
                        p.t.wait(svt, svt.n - 1)
                    for k in range(c.KH):
                        p.t.op("matmul", vp, zTa[:, k, r * 128:(r + 1) * 128],
                               wvsl[:, k, :], start=(k == 0), stop=False)
                    p.t.op("matmul", vp, ones_row_bf[:, :],
                           bvr_sb[0:1, g * GW:(g + 1) * GW],
                           start=False, stop=True, inc=[(st, 1)])
                    p.s.wait(st, st.n)
                    p.s.op("copy", vTg[:, r, :], vp, inc=[(svt, 1)])
                for hh in range(c.GH):
                    hd = g * c.GH + hh
                    qslot = hd % 2
                    if hd == 0:
                        issue_slices(0)
                    for which, wsl, wsem, reg, bias in (
                            (0, wsq, swq, qr, bqc_sb), (1, wsk, wsk_dummy, kr, bkc_sb)):
                        wsem = swq if which == 0 else swk
                        p.t.wait(wsem[qslot], wsem[qslot].n)
                        p.t.wait(svt, svt.n)
                        p.t.wait(sex, sex.n)
                        for qc in range(c.NQC):
                            pp = pbank(proj_ctr[0] % 2, c.QC)
                            proj_ctr[0] += 1
                            p.t.wait(sqp, sqp.n - 1)
                            for k in range(c.KH):
                                p.t.op("matmul", pp, wsl[qslot][:, k, :],
                                       zTa[:, k, qc * c.QC:(qc + 1) * c.QC],
                                       start=(k == 0), stop=(k == c.KH - 1),
                                       inc=[(st, 1)] if k == c.KH - 1 else None)
                            p.s.wait(st, st.n)
                            p.s.op("activation", reg[qslot][:, qc * c.QC:(qc + 1) * c.QC],
                                   pp, AF.Identity, bias=bias[:, hd:hd + 1],
                                   inc=[(sqp, 1)])
                        CH = min(512, c.S)
                        for ch in range(c.S // CH):
                            cols = slice(ch * CH, (ch + 1) * CH)
                            colabs = slice(seq * 0 + ch * CH, ch * CH + CH)
                            rp = pbank(6 + ch % 2, CH)
                            p.t.wait(sqp, sqp.n)
                            p.t.wait(sbc, sbc.n)
                            p.t.wait(sv, sv.n)
                            p.t.op("matmul", rp, rtbf[:, :], reg[qslot][:, cols],
                                   start=True, stop=True, inc=[(st, 1)])
                            p.v.wait(st, st.n)
                            vh("tensor_mul", tmp1[:, 0:CH], reg[qslot][:, cols],
                               ropec_sb[:, cols])
                            vh("tensor_tensor", tmp2[:, 0:CH], rp, ropes_sb[:, cols],
                               ALU.mult)
                            vh("tensor_add", reg[qslot][:, cols], tmp1[:, 0:CH],
                               tmp2[:, 0:CH])
                    if hd + 1 < c.NH:
                        issue_slices(hd + 1)
                    p.t.wait(sv, sv.n)
                    for qc in range(c.NQC):
                        aslot = ao_ctr[0] % 2
                        eslot = aslot
                        sump = pbank(6, c.QC)[0:1, :]
                        for kt in range(c.KT):
                            sp = pbank(kt % 4, c.QC)
                            p.t.wait(sex, sex.n - 3)
                            p.t.op("matmul", sp, kr[qslot][:, kt * 128:(kt + 1) * 128],
                                   qr[qslot][:, qc * c.QC:(qc + 1) * c.QC],
                                   start=True, stop=True, inc=[(st, 1)])
                            p.s.wait(st, st.n)
                            p.s.op("activation", expT[eslot][:, kt, :], sp, AF.Exp,
                                   scale=float(1.0 / math.sqrt(128.0)),
                                   inc=[(sex, 1)])
                            p.t.wait(sex, sex.n)
                            if kt == 0:
                                p.t.wait(sv, sv.n)
                            p.t.op("matmul", sump, ones_col_bf[:, :],
                                   expT[eslot][:, kt, :],
                                   start=(kt == 0), stop=(kt == c.KT - 1),
                                   inc=[(st, 1)] if kt == c.KT - 1 else None)
                        avp = pbank(4 + aslot, c.QC)
                        p.t.wait(saov, saov.n - 1)
                        for kt in range(c.KT):
                            p.t.op("matmul", avp, vTg[:, kt, hh * 128:(hh + 1) * 128],
                                   expT[eslot][:, kt, :],
                                   start=(kt == 0), stop=(kt == c.KT - 1),
                                   inc=[(st, 1)] if kt == c.KT - 1 else None)
                        p.v.wait(st, st.n)
                        vh("tensor_copy", sums_sb[:, :], sump)
                        vh("reciprocal", rsum[:, :], sums_sb[:, :])
                        p.t.wait(sv, sv.n)
                        p.t.wait(sbc, sbc.n)
                        bcp = pbank(7, c.QC)
                        p.t.op("matmul", bcp, ones_row_f[:, :], rsum[:, :],
                               start=True, stop=True, inc=[(st, 1)])
                        p.s.wait(st, st.n)
                        p.s.op("copy", bc_sb[:, :], bcp, inc=[(sbc, 1)])
                        p.v.wait(sbc, sbc.n)
                        p.v.wait(sao[aslot], sao[aslot].n)
                        p.v.op("tensor_tensor", ao_sb[aslot][:, :], avp,
                               bc_sb[:, :], ALU.mult, inc=[(saov, 1), (sv, 1)])
                        p.v.wait(sv, sv.n)
                        p.y.wait(saov, saov.n)
                        p.y.op("dma_start",
                               out=dao[qc, :, hd * c.QC:(hd + 1) * c.QC],
                               in_=ao_sb[aslot][:, :], inc=[(sao[aslot], 16)])
                        ao_ctr[0] += 1

        def emit_oproj(seq):
            p.sb_reset("attn_oproj")
            aoin = [p.sb(f"{tag}_aoin{i}_{seq}", [128, c.KH, c.QC], BF16) for i in range(2)]
            wot_sb = p.sb(f"{tag}_wot_{seq}", [128, c.KH, c.H], BF16)
            o_sb = p.sb(f"{tag}_osb_{seq}", [128, c.IT, c.QC], F32)
            xo = [p.sb(f"{tag}_xo{i}_{seq}", [128, c.H], F32) for i in range(2)]
            base = seq * c.S
            p.g.op("dma_start",
                   out=wot_sb.ap().rearrange("p a b -> p (a b)"),
                   in_=dwo, inc=[(swo, 16)])
            gsz = min(4, c.IT)
            for tc in range(c.NQC):
                islot = tc % 2
                p.y.wait(sop, sop.n)
                p.y.op("dma_start",
                       out=aoin[islot].ap().rearrange("p a b -> p (a b)"),
                       in_=dao[tc], inc=[(sai, 16)])
                p.t.wait(sai, sai.n)
                p.t.wait(swo, swo.n)
                for it in range(c.IT):
                    op_ = pbank(it % 2, c.QC)
                    if it >= 2:
                        p.t.wait(sop, sop.n - 1)
                    for k in range(c.KH):
                        p.t.op("matmul", op_, wot_sb[:, k, it * 128:(it + 1) * 128],
                               aoin[islot][:, k, :],
                               start=(k == 0), stop=(k == c.KH - 1),
                               inc=[(st, 1)] if k == c.KH - 1 else None)
                    p.s.wait(st, st.n)
                    p.s.op("copy", o_sb[:, it, :], op_, inc=[(sop, 1)])
                for ttl in range(c.QC // 128):
                    t0 = base + tc * c.QC + ttl * 128
                    slot = ttl % 2
                    p.guard_wait(p.y, ("ax", slot))
                    p.y.op("dma_start", out=xr[slot][:, :], in_=src[t0:t0 + 128, :],
                           inc=[(sx[slot], 16)])
                    p.v.wait(sxo[slot], sxo[slot].n)
                    p.t.wait(sop, sop.n)
                    for grp in range(c.IT // gsz):
                        bk = 2 + (sdr.n % 2)
                        p.t.wait(sdr, sdr.n - 1)
                        for q in range(gsz):
                            it = grp * gsz + q
                            p.t.op("transpose", pbank(bk, 128, q * 128),
                                   o_sb[:, it, ttl * 128:(ttl + 1) * 128], ident32[:, :],
                                   inc=[(st, 1)])
                        p.v.wait(st, st.n)
                        p.v.wait(sx[slot], sx[slot].n)
                        span = gsz * 128
                        p.v.op("tensor_tensor", xo[slot][:, grp * span:(grp + 1) * span],
                               xr[slot][:, grp * span:(grp + 1) * span], pbank(bk, span),
                               ALU.add, inc=[(sdr, 1)])
                    p.guard_set(("ax", slot), (sdr, sdr.n))
                    p.y.wait(sdr, sdr.n)
                    p.y.op("dma_start", out=dst[t0:t0 + 128, :], in_=xo[slot][:, :],
                           inc=[(sxo[slot], 16)])

        for seq in range(c.Bc):
            emit_ln(seq)
            emit_heads(seq)
            p.y.wait(sao[0], sao[0].n)
            p.y.wait(sao[1], sao[1].n)
            p.barrier()
            emit_oproj(seq)
            p.y.wait(sxo[0], sxo[0].n)
            p.y.wait(sxo[1], sxo[1].n)
            p.barrier()

    def emit_attn_stub(tag, src, dst):
        scp = p.sem(f"{tag}_cp")
        buf = p.sb(f"{tag}_cpbuf", [128, c.H], F32)
        for t0 in range(0, c.N, 128):
            p.y.wait(scp, scp.n)
            p.y.op("dma_start", out=buf[:, :], in_=src[t0:t0 + 128, :], inc=[(scp, 16)])
            p.y.wait(scp, scp.n)
            p.y.op("dma_start", out=dst[t0:t0 + 128, :], in_=buf[:, :], inc=[(scp, 16)])
        p.barrier()

    cur = dx
    if c.do_ffn1:
        emit_ffn("f1", cur, dx1 if (c.do_attn or c.do_ffn2) else dout)
        cur = dx1 if (c.do_attn or c.do_ffn2) else dout
    if c.do_attn:
        ad = dx2 if c.do_ffn2 else dout
        emit_attn("at", cur, ad)
        cur = ad
    if c.do_ffn2:
        emit_ffn("f2", cur, dout)
        cur = dout
    if cur is not dout:
        emit_attn_stub("fin", cur, dout)

    for name, s in p.sems.items():
        if "sxo" in name or name.endswith("_cp"):
            p.y.wait(s, s.n)
    p.replay()
    return nc


def make_consts(cfg: Cfg):
    c = cfg
    ident = np.eye(128, dtype=np.float32)
    R = np.zeros((128, 128), np.float32)
    R[:64, 64:] = -np.eye(64)
    R[64:, :64] = np.eye(64)
    RT = np.ascontiguousarray(R.T)
    inv = 1.0 / (c.rope_theta ** (np.arange(0, 128, 2, dtype=np.float32) / 128))
    fr = np.outer(inv, np.arange(c.S, dtype=np.float32))
    cs = np.cos(fr).astype(np.float32)
    sn = np.sin(fr).astype(np.float32)
    ropec = np.ascontiguousarray(np.concatenate([cs, cs], 0))
    ropes = np.ascontiguousarray(np.concatenate([sn, sn], 0))
    iota32 = np.ascontiguousarray(np.tile(np.arange(32, dtype=np.float32), (128, 1)))
    iotam = np.ascontiguousarray(np.tile(np.arange(32, dtype=np.float32), (128, 4)))
    return dict(ident=ident, rt=RT, ropec=ropec, ropes=ropes, iota32=iota32, iotam=iotam)


def host_inputs(cfg, x_core, Wq, bq, Wk, bk, Wv, bv, Wo, gw, gup, dwn, sim):
    c = cfg
    consts = make_consts(cfg)
    sim32 = np.zeros((32, 32), np.float32)
    sim32[:sim.shape[0], :sim.shape[1]] = sim
    GW = c.GH * 128

    def qk_tile(W):
        t = W.T.reshape(c.KH, 128, c.NH, 128).transpose(2, 1, 0, 3)
        return np.ascontiguousarray(t.reshape(c.NH, 128, c.KH * 128))

    wv_t = Wv.T.reshape(c.KH, 128, c.NG, GW).transpose(2, 1, 0, 3)
    wo_t = Wo.T.reshape(c.KH, 128, c.H).transpose(1, 0, 2)
    gw_t = gw.T.reshape(c.KH, 128, c.E).transpose(1, 0, 2)
    wgu_t = gup.reshape(c.E, 2, c.FT, 128, c.KH, 128).transpose(0, 2, 5, 4, 1, 3)
    wd_t = dwn.reshape(c.E, c.NWD, c.WDC * 128, c.FT, 128).transpose(0, 1, 4, 3, 2)
    return {
        "x": np.ascontiguousarray(x_core.reshape(c.N, c.H)),
        "wqt": qk_tile(Wq),
        "wkt": qk_tile(Wk),
        "wvt": np.ascontiguousarray(wv_t.reshape(c.NG, 128, c.KH * GW)),
        "wot": np.ascontiguousarray(wo_t.reshape(128, c.KH * c.H)),
        "bqc": np.ascontiguousarray(bq.reshape(c.KH, 128).T),
        "bkc": np.ascontiguousarray(bk.reshape(c.KH, 128).T),
        "bvr": np.ascontiguousarray(bv.reshape(1, c.H)),
        "gwt": np.ascontiguousarray(gw_t),
        "wgut": np.ascontiguousarray(wgu_t.reshape(c.E * c.FT, 128, c.KH * 2 * 128)),
        "wdt": np.ascontiguousarray(wd_t.reshape(c.E * c.NWD, 128, c.FT * c.WDC * 128)),
        "sim": sim32,
        **consts,
    }


def kernel(L0_attn_hidden_states, L0_attn_Wq, L0_attn_bq, L0_attn_Wk, L0_attn_bk,
           L0_attn_Wv, L0_attn_bv, L0_attn_Wo, L0_ffn_gate_weight,
           L0_ffn_gate_up_proj, L0_ffn_down_proj, L0_ffn_similarity):
    cfg = Cfg()
    x = np.asarray(L0_attn_hidden_states, dtype=np.float32)
    B = x.shape[0]
    ncores = 8
    per = B // ncores
    nc = build_nc(cfg)
    shared = host_inputs(
        cfg, x[0:per],
        np.asarray(L0_attn_Wq, np.float32), np.asarray(L0_attn_bq, np.float32),
        np.asarray(L0_attn_Wk, np.float32), np.asarray(L0_attn_bk, np.float32),
        np.asarray(L0_attn_Wv, np.float32), np.asarray(L0_attn_bv, np.float32),
        np.asarray(L0_attn_Wo, np.float32),
        np.asarray(L0_ffn_gate_weight, np.float32),
        np.asarray(L0_ffn_gate_up_proj, np.float32),
        np.asarray(L0_ffn_down_proj, np.float32),
        np.asarray(L0_ffn_similarity, np.float32))
    in_maps = []
    for ci in range(ncores):
        m = dict(shared)
        m["x"] = np.ascontiguousarray(
            x[ci * per:(ci + 1) * per].reshape(cfg.N, cfg.H))
        in_maps.append(m)
    res = run_bass_kernel_spmd(nc, in_maps, core_ids=list(range(ncores)))
    outs = [r["out"].reshape(per, cfg.S, cfg.H) for r in res.results]
    return np.concatenate(outs, axis=0)


# version 07
# speedup vs baseline: 1.0834x; 1.0834x over previous
import math
from dataclasses import dataclass

import numpy as np

import concourse.bass as bass
import concourse.mybir as mybir
from concourse.bass_utils import run_bass_kernel_spmd

F32 = mybir.dt.float32
BF16 = mybir.dt.bfloat16
U32 = mybir.dt.uint32
AF = mybir.ActivationFunctionType
ALU = mybir.AluOpType
AX = mybir.AxisListType


@dataclass
class Cfg:
    Bc: int = 2
    S: int = 2048
    H: int = 2048
    NH: int = 16
    E: int = 32
    F: int = 768
    Tb: int = 1024
    rope_theta: float = 1e6
    eps: float = 1e-5
    do_ffn1: bool = True
    do_attn: bool = True
    do_ffn2: bool = True
    debug: bool = False

    @property
    def N(self): return self.Bc * self.S
    @property
    def KH(self): return self.H // 128
    @property
    def FT(self): return self.F // 128
    @property
    def IT(self): return self.H // 128
    @property
    def NBLK(self): return self.N // self.Tb
    @property
    def TT(self): return self.Tb // 128
    @property
    def TC(self): return min(512, self.Tb)
    @property
    def NTC(self): return self.Tb // self.TC
    @property
    def QC(self): return min(512, self.S)
    @property
    def NQC(self): return self.S // self.QC
    @property
    def KT(self): return self.S // 128
    @property
    def GH(self): return min(4, self.NH)
    @property
    def NG(self): return self.NH // self.GH
    @property
    def ST(self): return self.S // 128
    @property
    def WDC(self): return min(4, self.IT)
    @property
    def NWD(self): return self.IT // self.WDC


TOPK = 8
SSEL = 4
THRESH = 0.5


class Sem:
    __slots__ = ("h", "n", "name")

    def __init__(self, nc, name):
        self.h = nc.alloc_semaphore(name=name)
        self.n = 0
        self.name = name


class Stream:
    def __init__(self, name):
        self.name = name
        self.ops = []
        self._waited = {}

    def wait(self, sem, val):
        if val <= 0:
            return
        if val > sem.n:
            raise RuntimeError(
                f"{self.name}: wait_ge({sem.name},{val}) exceeds emitted total {sem.n}")
        if self._waited.get(id(sem), -1) >= val:
            return
        self._waited[id(sem)] = val
        self.ops.append(lambda e, h=sem.h, v=val: e.wait_ge(h, v))

    def op(self, meth, *args, inc=None, **kw):
        incs = []
        if inc:
            for sem, amt in inc:
                sem.n += amt
                incs.append((sem.h, amt))

        def f(e, meth=meth, args=args, kw=kw, incs=incs):
            ins = getattr(e, meth)(*args, **kw)
            if incs:
                ins.then_inc(*incs[0])
                for h, amt in incs[1:]:
                    e.nop().then_inc(h, amt)
        self.ops.append(f)


class Prog:
    def __init__(self, nc):
        self.nc = nc
        self.t = Stream("tensor")
        self.v = Stream("vector")
        self.s = Stream("scalar")
        self.g = Stream("gpsimd")
        self.y = Stream("sync")
        self.streams = [self.t, self.v, self.s, self.g, self.y]
        self.bar = Sem(nc, "bar")
        self._sb_cursor = 16512
        self._sb_marks = {}
        self.sems = {}
        self.guards = {}

    def sem(self, name):
        if name not in self.sems:
            self.sems[name] = Sem(self.nc, name)
        return self.sems[name]

    def sb(self, name, shape, dtype, align=64):
        sz = int(np.prod(shape[1:])) * mybir.dt.size(dtype)
        if sz < 64:
            align = 32
        self._sb_cursor = (self._sb_cursor + align - 1) // align * align
        t = self.nc.alloc_sbuf_tensor_at(name, list(shape), dtype, offset=self._sb_cursor)
        self._sb_cursor += sz
        assert self._sb_cursor <= 229344, f"SBUF overflow at {name}: {self._sb_cursor}"
        return t

    def sb_mark(self, tag):
        self._sb_marks[tag] = self._sb_cursor

    def sb_reset(self, tag):
        self._sb_cursor = self._sb_marks[tag]

    def barrier(self):
        targets = [(s, s.n) for s in self.sems.values() if s.n > 0]
        for st in self.streams:
            for s, v in targets:
                st.wait(s, v)

    def guard_set(self, key, *semvals):
        self.guards[key] = list(semvals)

    def guard_wait(self, stream, key):
        for sem, val in self.guards.get(key, ()):
            stream.wait(sem, val)

    def replay(self):
        nc = self.nc
        with nc.Block() as block:
            @block.tensor
            def _(e):
                for f in self.t.ops:
                    f(e)

            @block.vector
            def _(e):
                for f in self.v.ops:
                    f(e)

            @block.scalar
            def _(e):
                for f in self.s.ops:
                    f(e)

            @block.gpsimd
            def _(e):
                for f in self.g.ops:
                    f(e)

            @block.sync
            def _(e):
                for f in self.y.ops:
                    f(e)


def build_nc(cfg: Cfg):
    nc = bass.Bass("TRN2", target_bir_lowering=False, debug=False)
    p = Prog(nc)
    c = cfg

    dx = nc.dram_tensor("x", [c.N, c.H], F32, kind="ExternalInput").ap()
    dwq = nc.dram_tensor("wqt", [c.NH, 128, c.KH * 128], F32, kind="ExternalInput").ap()
    dwk = nc.dram_tensor("wkt", [c.NH, 128, c.KH * 128], F32, kind="ExternalInput").ap()
    dwv = nc.dram_tensor("wvt", [c.NG, 128, c.KH * c.GH * 128], F32, kind="ExternalInput").ap()
    dwo = nc.dram_tensor("wot", [128, c.KH * c.H], F32, kind="ExternalInput").ap()
    dbqc = nc.dram_tensor("bqc", [128, c.KH], F32, kind="ExternalInput").ap()
    dbkc = nc.dram_tensor("bkc", [128, c.KH], F32, kind="ExternalInput").ap()
    dbvr = nc.dram_tensor("bvr", [1, c.H], F32, kind="ExternalInput").ap()
    dgw = nc.dram_tensor("gwt", [128, c.KH, c.E], F32, kind="ExternalInput").ap()
    dwgu = nc.dram_tensor("wgut", [c.E * c.FT, 128, c.KH * 2 * 128], F32,
                          kind="ExternalInput").ap()
    dwd = nc.dram_tensor("wdt", [c.E * c.NWD, 128, c.FT * c.WDC * 128], F32,
                         kind="ExternalInput").ap()
    dsim = nc.dram_tensor("sim", [32, 32], F32, kind="ExternalInput").ap()
    dident = nc.dram_tensor("ident", [128, 128], F32, kind="ExternalInput").ap()
    drt = nc.dram_tensor("rt", [128, 128], F32, kind="ExternalInput").ap()
    dropec = nc.dram_tensor("ropec", [128, c.S], F32, kind="ExternalInput").ap()
    dropes = nc.dram_tensor("ropes", [128, c.S], F32, kind="ExternalInput").ap()
    diota = nc.dram_tensor("iota32", [128, 32], F32, kind="ExternalInput").ap()
    diotam = nc.dram_tensor("iotam", [128, 128], F32, kind="ExternalInput").ap()

    dout = nc.dram_tensor("out", [c.N, c.H], F32, kind="ExternalOutput").ap()
    dx1 = nc.dram_tensor("x1s", [c.N, c.H], F32, kind="Internal").ap()
    dx2 = nc.dram_tensor("x2s", [c.N, c.H], F32, kind="Internal").ap()
    dao = nc.dram_tensor("aos", [c.NQC, 128, c.KH * c.QC], BF16, kind="Internal").ap()
    if c.debug:
        dbg_z = nc.dram_tensor("dbg_z", [c.N, c.H], F32, kind="ExternalOutput").ap()
        dbg_comb = nc.dram_tensor("dbg_comb", [c.NBLK, 32, c.Tb], F32, kind="ExternalOutput").ap()
        dbg_y = nc.dram_tensor("dbg_y", [c.NBLK, 128, c.IT * c.Tb], F32, kind="ExternalOutput").ap()
        dbg_rt = nc.dram_tensor("dbg_rt", [c.N, 80], F32, kind="ExternalOutput").ap()
        sdbg = p.sem("sdbg")

    pb = [nc.alloc_psum_tensor(f"pb{i}", [128, 512], F32) for i in range(8)]

    def pbank(i, ncols=512, off=0):
        return pb[i][:, off:off + ncols]

    def pbank_bf(i, ncols, off=0):
        return pb[i].ap().bitcast(BF16)[:, off:off + ncols]

    ident32 = p.sb("ident32", [128, 128], F32)
    identbf = p.sb("identbf", [128, 128], BF16)
    rtbf = p.sb("rtbf", [128, 128], BF16)
    iota32 = p.sb("iota32", [128, 32], F32)
    iotam = p.sb("iotam", [128, 128], F32)
    gwt_sb = p.sb("gwt_sb", [128, c.KH, c.E], F32)
    simbd = p.sb("simbd", [128, 128], F32)
    bqc_sb = p.sb("bqc_sb", [128, c.KH], F32)
    bkc_sb = p.sb("bkc_sb", [128, c.KH], F32)
    bvr_sb = p.sb("bvr_sb", [1, c.H], BF16)
    ones_col_bf = p.sb("ones_col_bf", [128, 1], BF16)
    ones_row_bf = p.sb("ones_row_bf", [1, 128], BF16)
    ones_row_f = p.sb("ones_row_f", [1, 128], F32)
    p.sb_mark("phase")

    sd = p.sem("sd")
    sg0 = p.sem("sg0")
    sv0 = p.sem("sv0")

    p.y.op("dma_start", out=ident32[:, :], in_=dident, inc=[(sd, 16)])
    p.y.op("dma_start", out=iota32[:, :], in_=diota, inc=[(sd, 16)])
    p.y.op("dma_start", out=iotam[:, :], in_=diotam, inc=[(sd, 16)])
    p.y.op("dma_start", out=gwt_sb[:, :, :], in_=dgw, inc=[(sd, 16)])
    p.y.op("dma_start", out=bqc_sb[:, :], in_=dbqc, inc=[(sd, 16)])
    p.y.op("dma_start", out=bkc_sb[:, :], in_=dbkc, inc=[(sd, 16)])
    p.g.op("dma_start", out=identbf[:, :], in_=dident, inc=[(sg0, 16)])
    p.g.op("dma_start", out=rtbf[:, :], in_=drt, inc=[(sg0, 16)])
    p.g.op("dma_start", out=bvr_sb[:, :], in_=dbvr, inc=[(sg0, 16)])
    p.v.op("memset", simbd[:, :], 0.0, inc=[(sv0, 1)])
    p.y.wait(sv0, 1)
    for b in range(4):
        p.y.op("dma_start", out=simbd[32 * b:32 * (b + 1), 32 * b:32 * (b + 1)],
               in_=dsim, inc=[(sd, 16)])
    p.v.op("memset", ones_col_bf[:, :], 1.0)
    p.v.op("memset", ones_row_bf[:, :], 1.0)
    p.v.op("memset", ones_row_f[:, :], 1.0)
    p.y.wait(sd, sd.n)
    p.g.wait(sg0, sg0.n)
    p.barrier()

    def emit_ffn(tag, src, dst):
        p.sb_reset("phase")
        _zT0 = p.sb(f"{tag}_zT0", [128, c.KH, c.Tb], BF16)
        zT = [_zT0, _zT0]
        yT = p.sb(f"{tag}_y", [128, c.TT, c.H], F32)
        wgu = [p.sb(f"{tag}_wgu{i}", [128, c.KH, 2, 128], BF16) for i in range(2)]
        wd = [p.sb(f"{tag}_wd{i}", [128, c.FT, c.WDC * 128], BF16) for i in range(2)]
        hsb = [p.sb(f"{tag}_h{i}", [128, c.FT, c.TC], BF16) for i in range(2)]
        xr = [p.sb(f"{tag}_xr{i}", [128, c.H], F32) for i in range(2)]
        xo = [p.sb(f"{tag}_xo{i}", [128, c.H], F32) for i in range(2)]
        _z320 = p.sb(f"{tag}_z320", [128, c.H], F32)
        z32 = [_z320, _z320]
        _zbf0 = p.sb(f"{tag}_zbf0", [128, c.H], BF16)
        zbf = [_zbf0, _zbf0]
        sq = xo[0]
        zfT = [p.sb(f"{tag}_zfT{i}", [128, 128], F32) for i in range(2)]
        comb_blk = p.sb(f"{tag}_combblk", [128, c.TT, 32], F32)
        silu_sb = p.sb(f"{tag}_silu", [128, c.TC], F32)
        lgs = p.sb(f"{tag}_lg", [128, 32], F32)
        topv = p.sb(f"{tag}_topv", [128, 8], F32)
        topi = p.sb(f"{tag}_topi", [128, 8], U32)
        topif = p.sb(f"{tag}_topif", [128, 8], F32)
        nl0 = p.sb(f"{tag}_nl0", [128, 1], F32)
        expw = p.sb(f"{tag}_expw", [128, 8], F32)
        sumw = p.sb(f"{tag}_sumw", [128, 1], F32)
        rcpw = p.sb(f"{tag}_rcpw", [128, 1], F32)
        rw = p.sb(f"{tag}_rw", [128, 8], F32)
        ohsec = p.sb(f"{tag}_ohsec", [128, 128], F32)
        ohsecT = p.sb(f"{tag}_ohsecT", [128, 128], F32)
        simrows = p.sb(f"{tag}_simrows", [128, 128], F32)
        ohrep = p.sb(f"{tag}_ohrep", [128, 128], F32)
        prod = ohsecT
        sc = p.sb(f"{tag}_sc", [128, 16], F32)
        bs = p.sb(f"{tag}_bs", [128, 4], F32)
        mx8 = simrows.ap().bitcast(F32)[:, 0:8]
        scpad = simrows[:, 8:16]
        ix8 = simrows.ap().bitcast(U32)[:, 16:24]
        bif = p.sb(f"{tag}_bif", [128, 4], F32)
        o4 = simrows[:, 24:28]
        t4 = simrows[:, 28:32]
        bp = simrows[:, 36:40]
        msk = simrows[:, 32:36]
        ns = simrows[:, 40:44]
        comb = simrows[:, 44:76]
        mstat = p.sb(f"{tag}_mstat", [128, 1], F32)
        ssq = p.sb(f"{tag}_ssq", [128, 1], F32)
        v1e = p.sb(f"{tag}_v1e", [128, 1], F32)
        mm2 = p.sb(f"{tag}_mm2", [128, 1], F32)
        varv = p.sb(f"{tag}_varv", [128, 1], F32)
        rcpv = p.sb(f"{tag}_rcpv", [128, 1], F32)
        rstd = p.sb(f"{tag}_rstd", [128, 1], F32)
        nmr = p.sb(f"{tag}_nmr", [128, 1], F32)

        sx = [p.sem(f"{tag}_sxA"), p.sem(f"{tag}_sxB")]
        sxo = [p.sem(f"{tag}_sxoA"), p.sem(f"{tag}_sxoB")]
        swg = [p.sem(f"{tag}_swgA"), p.sem(f"{tag}_swgB")]
        swd = [p.sem(f"{tag}_swdA"), p.sem(f"{tag}_swdB")]
        sv = p.sem(f"{tag}_sv")
        ss = p.sem(f"{tag}_ss")
        st = p.sem(f"{tag}_st")
        shp = p.sem(f"{tag}_shp")
        sgu = p.sem(f"{tag}_sgu")
        syp = p.sem(f"{tag}_syp")
        sye = p.sem(f"{tag}_sye")
        srt = p.sem(f"{tag}_srt")
        sdr = p.sem(f"{tag}_sdr")
        vself = p.sem(f"{tag}_vself")

        def vh(meth, *args, extra=None, **kw):
            inc = [(sv, 1)] + (extra or [])
            p.v.op(meth, *args, inc=inc, **kw)
            p.v.wait(sv, sv.n)

        def ln_router(blk, tt):
            par = blk % 2
            slot = tt % 2
            t0 = blk * c.Tb + tt * 128
            p.guard_wait(p.y, ("fx", slot))
            p.y.op("dma_start", out=xr[slot][:, :], in_=src[t0:t0 + 128, :],
                   inc=[(sx[slot], 16)])
            xa = xr[slot]
            p.v.wait(sx[slot], sx[slot].n)
            vh("reduce_sum", mstat[:, :], xa[:, :], axis=AX.X)
            p.s.wait(sx[slot], sx[slot].n)
            p.s.wait(sxo[0], sxo[0].n)
            p.s.op("activation", sq[:, :], xa[:, :], AF.Square, accum_out=ssq[:, :],
                   inc=[(ss, 1)])
            p.v.wait(ss, ss.n)
            vh("tensor_scalar", v1e[:, :], ssq[:, :], 1.0 / c.H, c.eps, ALU.mult, ALU.add)
            vh("tensor_scalar", mstat[:, :], mstat[:, :], 1.0 / c.H, None, ALU.mult)
            vh("tensor_mul", mm2[:, :], mstat[:, :], mstat[:, :])
            vh("tensor_sub", varv[:, :], v1e[:, :], mm2[:, :])
            vh("reciprocal", rcpv[:, :], varv[:, :])
            p.s.wait(sv, sv.n)
            p.s.op("activation", rstd[:, :], rcpv[:, :], AF.Sqrt, inc=[(ss, 1)])
            p.v.wait(ss, ss.n)
            vh("tensor_scalar", nmr[:, :], mstat[:, :], rstd[:, :], -1.0, ALU.mult, ALU.mult)
            p.guard_wait(p.v, ("z32",))
            vh("tensor_scalar", z32[slot][:, :], xa[:, :], rstd[:, :], nmr[:, :],
               ALU.mult, ALU.add)
            p.guard_set(("fx", slot), (sv, sv.n), (ss, ss.n))
            vh("tensor_copy", zbf[slot][:, :], z32[slot][:, :])
            if c.debug and tag == "f1":
                p.y.wait(sv, sv.n)
                p.y.op("dma_start", out=dbg_z[t0:t0 + 128, :], in_=z32[slot][:, :],
                       inc=[(p.sem("sdbg"), 16)])

            p.t.wait(sv, sv.n)
            lgp = pbank(7, c.E, 0)
            for k in range(c.KH):
                dst_bf = pbank_bf(k % 2, 128, 0)
                if k >= 2:
                    p.t.wait(sv, sv.n)
                p.t.op("transpose", dst_bf, zbf[slot][:, k * 128:(k + 1) * 128],
                       identbf[:, :], inc=[(st, 1)])
                p.v.wait(st, st.n)
                p.v.op("tensor_copy", zT[par][:, k, tt * 128:(tt + 1) * 128], dst_bf,
                       inc=[(sv, 1)])
                dst_f = pbank(2 + k % 2, 128, 0)
                if k >= 2:
                    p.t.wait(ss, ss.n)
                p.t.op("transpose", dst_f, z32[slot][:, k * 128:(k + 1) * 128],
                       ident32[:, :], inc=[(st, 1)])
                p.s.wait(st, st.n)
                p.s.op("copy", zfT[k % 2][:, :], dst_f, inc=[(ss, 1)])
                p.t.wait(ss, ss.n)
                p.t.op("matmul", lgp, zfT[k % 2][:, :], gwt_sb[:, k, :c.E],
                       start=(k == 0), stop=(k == c.KH - 1),
                       inc=[(st, 1)] if k == c.KH - 1 else None)

            p.guard_set(("z32",), (st, st.n))
            p.v.wait(st, st.n)
            if c.E < 32:
                vh("memset", lgs[:, c.E:], -1e30)
            vh("tensor_copy", lgs[:, :c.E], lgp)
            vh("max", out=topv[:, :], in_=lgs[:, :])
            vh("max_index", out=topi[:, :], in_max=topv[:, :], in_values=lgs[:, :])
            vh("tensor_copy", topif[:, :], topi[:, :])
            vh("tensor_scalar", nl0[:, :], topv[:, 0:1], -1.0, None, ALU.mult)
            p.s.wait(sv, sv.n)
            p.s.op("activation", expw[:, :], topv[:, :], AF.Exp, bias=nl0[:, :],
                   inc=[(ss, 1)])
            p.v.wait(ss, ss.n)
            vh("reduce_sum", sumw[:, :], expw[:, :], axis=AX.X)
            vh("reciprocal", rcpw[:, :], sumw[:, :])
            vh("tensor_scalar", rw[:, :], expw[:, :], rcpw[:, :], 0.5, ALU.mult, ALU.mult)
            for j in range(4):
                vh("tensor_scalar", ohsec[:, 32 * j:32 * (j + 1)], iota32[:, :],
                   topif[:, 4 + j:5 + j], None, ALU.is_equal)
            p.t.wait(sv, sv.n)
            ohp = pbank(4, 128, 0)
            p.t.op("transpose", ohp, ohsec[:, :], ident32[:, :], inc=[(st, 1)])
            p.s.wait(st, st.n)
            p.s.op("copy", ohsecT[:, :], ohp, inc=[(ss, 1)])
            p.t.wait(ss, ss.n)
            srp = pbank(5, 128, 0)
            p.t.op("matmul", srp, ohsecT[:, :], simbd[:, :], start=True, stop=True,
                   inc=[(st, 1)])
            p.v.wait(st, st.n)
            vh("tensor_copy", simrows[:, :], srp)
            for i in range(4):
                vh("tensor_scalar", ohrep[:, :], iotam[:, :], topif[:, i:i + 1],
                   None, ALU.is_equal)
                vh("tensor_mul", prod[:, :], simrows[:, :], ohrep[:, :])
                vh("tensor_reduce",
                   sc.ap().rearrange("p (j i) -> p j i", i=4)[:, :, i:i + 1],
                   prod.ap().rearrange("p (j e) -> p j e", e=32), AX.X, ALU.add)
            vh("memset", scpad[:, :], -1e30)
            for j in range(4):
                vh("tensor_copy", scpad[:, 0:4], sc[:, 4 * j:4 * (j + 1)])
                vh("max", out=mx8[:, :], in_=scpad[:, :])
                vh("max_index", out=ix8[:, :], in_max=mx8[:, :], in_values=scpad[:, :])
                vh("tensor_copy", bs[:, j:j + 1], mx8[:, 0:1])
                vh("tensor_copy", bif[:, j:j + 1], ix8[:, 0:1])
            for j in range(4):
                vh("tensor_scalar", o4[:, :], iota32[:, 0:4], bif[:, j:j + 1],
                   None, ALU.is_equal)
                vh("tensor_mul", t4[:, :], o4[:, :], topif[:, 0:4])
                vh("reduce_sum", bp[:, j:j + 1], t4[:, :], axis=AX.X)
            vh("tensor_scalar", msk[:, :], bs[:, :], THRESH, None, ALU.is_ge)
            vh("tensor_sub", t4[:, :], bp[:, :], topif[:, 4:8])
            vh("tensor_mul", t4[:, :], msk[:, :], t4[:, :])
            vh("tensor_add", ns[:, :], topif[:, 4:8], t4[:, :])
            for j in range(8):
                idx = topif[:, j:j + 1] if j < 4 else ns[:, j - 4:j - 3]
                vh("tensor_scalar", ohrep[:, 0:32], iota32[:, :], idx, None,
                   ALU.is_equal)
                if j == 0:
                    vh("tensor_scalar", comb[:, :], ohrep[:, 0:32], rw[:, 0:1],
                       None, ALU.mult)
                else:
                    vh("scalar_tensor_tensor", comb[:, :], ohrep[:, 0:32],
                       rw[:, j:j + 1], comb[:, :], ALU.mult, ALU.add)
            vh("tensor_copy", comb_blk[:, tt, :], comb[:, :], extra=[(srt, 1)])
            if c.debug and tag == "f1":
                dbgv = p.sb(f"{tag}_dbgrt_{blk}_{tt}", [128, 80], F32)
                p.v.op("tensor_copy", dbgv[:, 0:32], lgs[:, :])
                p.v.op("tensor_copy", dbgv[:, 32:40], topv[:, :])
                p.v.op("tensor_copy", dbgv[:, 40:48], topif[:, :])
                p.v.op("tensor_copy", dbgv[:, 48:56], rw[:, :])
                p.v.op("tensor_copy", dbgv[:, 56:60], bs[:, :])
                p.v.op("tensor_copy", dbgv[:, 60:64], bif[:, :])
                p.v.op("tensor_copy", dbgv[:, 64:68], bp[:, :])
                p.v.op("tensor_copy", dbgv[:, 68:72], ns[:, :])
                p.v.op("tensor_copy", dbgv[:, 72:76], sc[:, 0:4])
                p.v.op("tensor_copy", dbgv[:, 76:80], sc[:, 4:8], inc=[(p.sem("sdv"), 1)])
                p.y.wait(p.sem("sdv"), p.sem("sdv").n)
                p.y.op("dma_start", out=dbg_rt[t0:t0 + 128, :], in_=dbgv[:, :],
                       inc=[(p.sem("sdbg"), 16)])

        pair_ctr = [0]
        silu_val = {}
        setfree = {}
        nchunks_gu = c.NBLK * c.E * c.FT
        nchunks_wd = c.NBLK * c.E * c.NWD
        wgu_free = {}
        wd_free = {}
        srt_block = {}

        def issue_wgu(gc):
            if gc >= nchunks_gu:
                return
            e = (gc // c.FT) % c.E
            j = gc % c.FT
            slot = gc % 2
            if gc - 2 in wgu_free:
                p.g.wait(ss, wgu_free[gc - 2])
            p.g.op("dma_start",
                   out=wgu[slot].ap().rearrange("p k s f -> p (k s f)"),
                   in_=dwgu[e * c.FT + j], inc=[(swg[slot], 16)])

        def issue_wd(gd):
            if gd >= nchunks_wd:
                return
            e = (gd // c.NWD) % c.E
            gch = gd % c.NWD
            slot = gd % 2
            if gd - 2 in wd_free:
                p.g.wait(sye, wd_free[gd - 2])
            p.g.op("dma_start",
                   out=wd[slot].ap().rearrange("p a b -> p (a b)"),
                   in_=dwd[e * c.NWD + gch], inc=[(swd[slot], 16)])

        def acquire_set(ctr, width):
            for sem, val in setfree.get(ctr - 2, ()):
                p.t.wait(sem, val)
            b = 0 if ctr % 2 == 0 else 4
            return [pbank(b + i, width) for i in range(4)]

        def experts(blk):
            par = blk % 2
            p.t.wait(sv, sv.n)
            p.t.wait(ss, ss.n)
            srt_block[blk] = srt.n
            for e in range(c.E):
                for j in range(c.FT):
                    gc = (blk * c.E + e) * c.FT + j
                    wslot = gc % 2
                    p.t.wait(swg[wslot], swg[wslot].n)
                    for tc in range(c.NTC):
                        pi = pair_ctr[0]
                        if pi - 2 in silu_val:
                            p.t.wait(ss, silu_val[pi - 2])
                        banks = acquire_set(pi, c.TC)
                        gp, up = banks[0], banks[1]
                        for k in range(c.KH):
                            last = (k == c.KH - 1)
                            p.t.op("matmul", gp, wgu[wslot][:, k, 0, :],
                                   zT[par][:, k, tc * c.TC:(tc + 1) * c.TC],
                                   start=(k == 0), stop=last)
                            p.t.op("matmul", up, wgu[wslot][:, k, 1, :],
                                   zT[par][:, k, tc * c.TC:(tc + 1) * c.TC],
                                   start=(k == 0), stop=last,
                                   inc=[(sgu, 1)] if last else None)
                        hslot = tc % 2
                        p.s.wait(sgu, sgu.n)
                        p.s.wait(shp, shp.n)
                        p.s.op("activation", silu_sb[:, :], gp, AF.Silu,
                               inc=[(ss, 1)])
                        p.v.wait(ss, ss.n)
                        p.guard_wait(p.v, ("h", hslot))
                        p.v.op("tensor_tensor", hsb[hslot][:, j, :], silu_sb[:, :],
                               up, ALU.mult, inc=[(shp, 1)])
                        silu_val[pi] = ss.n
                        setfree[pi] = [(ss, ss.n), (shp, shp.n)]
                        pair_ctr[0] += 1
                    wgu_free[gc] = ss.n
                    issue_wgu(gc + 2)

                p.v.wait(srt, srt_block[blk])
                for gch in range(c.NWD):
                    gd = (blk * c.E + e) * c.NWD + gch
                    wslot = gd % 2
                    p.t.wait(swd[wslot], swd[wslot].n)
                    for tc in range(c.NTC):
                        hslot = tc % 2
                        p.t.wait(shp, shp.n - (c.NTC - 1 - tc))
                        ICW = min(512, c.WDC * 128)
                        NIC = (c.WDC * 128) // ICW
                        for tq in range(c.TC // 128):
                            ttile = tc * (c.TC // 128) + tq
                            ctr = pair_ctr[0]
                            ybanks = acquire_set(ctr, ICW)
                            ic0 = gch * c.WDC * 128
                            for ft in range(c.FT):
                                for icq in range(NIC):
                                    p.t.op("matmul", ybanks[icq],
                                           hsb[hslot][:, ft, tq * 128:(tq + 1) * 128],
                                           wd[wslot][:, ft, icq * ICW:(icq + 1) * ICW],
                                           start=(ft == 0), stop=(ft == c.FT - 1),
                                           inc=[(syp, 1)] if ft == c.FT - 1 else None)
                            p.v.wait(syp, syp.n)
                            cw = comb_blk[:, ttile, e:e + 1]
                            for icq in range(NIC):
                                ycol = yT[:, ttile,
                                          ic0 + icq * ICW:ic0 + (icq + 1) * ICW]
                                if e == 0:
                                    p.v.op("tensor_scalar", ycol, ybanks[icq],
                                           cw, None, ALU.mult, inc=[(sye, 1)])
                                else:
                                    p.v.op("scalar_tensor_tensor", ycol,
                                           ybanks[icq], cw, ycol,
                                           ALU.mult, ALU.add, inc=[(sye, 1)])
                            if gch == c.NWD - 1 and tq == c.TC // 128 - 1:
                                p.guard_set(("h", hslot), (sye, sye.n))
                            setfree[ctr] = [(sye, sye.n)]
                            pair_ctr[0] += 1
                    wd_free[gd] = sye.n
                    issue_wd(gd + 2)

        def drain(blk):
            p.v.wait(ss, ss.n)
            for tt in range(c.TT):
                t0 = blk * c.Tb + tt * 128
                slot = tt % 2
                oslot = tt % 2
                p.guard_wait(p.y, ("fx", slot))
                p.y.op("dma_start", out=xr[slot][:, :], in_=src[t0:t0 + 128, :],
                       inc=[(sx[slot], 16)])
                p.v.wait(sxo[oslot], sxo[oslot].n)
                p.v.wait(sx[slot], sx[slot].n)
                p.v.op("tensor_tensor", xo[oslot][:, :], xr[slot][:, :],
                       yT[:, tt, :], ALU.add, inc=[(sdr, 1)])
                p.guard_set(("fx", slot), (sdr, sdr.n))
                p.y.wait(sdr, sdr.n)
                p.y.op("dma_start", out=dst[t0:t0 + 128, :], in_=xo[oslot][:, :],
                       inc=[(sxo[oslot], 16)])

        for gci in range(2):
            issue_wgu(gci)
            issue_wd(gci)
        for blk in range(c.NBLK):
            for tt in range(c.TT):
                ln_router(blk, tt)
            experts(blk)
            drain(blk)
        p.y.wait(sxo[0], sxo[0].n)
        p.y.wait(sxo[1], sxo[1].n)
        p.barrier()

    def emit_attn(tag, src, dst):
        p.sb_reset("phase")
        GW = c.GH * 128
        xr = [p.sb(f"{tag}_xr{i}", [128, c.H], F32) for i in range(2)]
        ropec_sb = p.sb(f"{tag}_ropec", [128, c.S], BF16)
        ropes_sb = p.sb(f"{tag}_ropes", [128, c.S], BF16)
        p.sb_mark("attn_oproj")
        zTa = p.sb(f"{tag}_zT", [128, c.KH, c.S], BF16)
        vTg = p.sb(f"{tag}_vTg", [128, c.ST, GW], BF16)
        qr = [p.sb(f"{tag}_qr{i}", [128, c.S], BF16) for i in range(2)]
        kr = [p.sb(f"{tag}_kr{i}", [128, c.S], BF16) for i in range(2)]
        wsq = [p.sb(f"{tag}_wsq{i}", [128, c.KH, 128], BF16) for i in range(2)]
        wsk = [p.sb(f"{tag}_wsk{i}", [128, c.KH, 128], BF16) for i in range(2)]
        wvsl = p.sb(f"{tag}_wvsl", [128, c.KH, GW], BF16)
        expT = [p.sb(f"{tag}_expT{i}", [128, c.KT, c.QC], BF16) for i in range(2)]
        tmp1 = p.sb(f"{tag}_tmp1", [128, 512], BF16)
        tmp2 = p.sb(f"{tag}_tmp2", [128, 512], BF16)
        sums_sb = p.sb(f"{tag}_sums", [1, c.QC], F32)
        rsum = p.sb(f"{tag}_rsum", [1, c.QC], F32)
        bc_sb = p.sb(f"{tag}_bc", [128, c.QC], F32)
        ao_sb = [p.sb(f"{tag}_ao{i}", [128, c.QC], BF16) for i in range(2)]
        _azbf = p.sb(f"{tag}_zbf0", [128, c.H], BF16)
        zbf = [_azbf, _azbf]
        sq = expT[0].ap().rearrange("p k q -> p (k q)").bitcast(F32)[:, 0:c.H]
        mstat = p.sb(f"{tag}_mstat", [128, 1], F32)
        ssq = p.sb(f"{tag}_ssq", [128, 1], F32)
        v1e = p.sb(f"{tag}_v1e", [128, 1], F32)
        mm2 = p.sb(f"{tag}_mm2", [128, 1], F32)
        varv = p.sb(f"{tag}_varv", [128, 1], F32)
        rcpv = p.sb(f"{tag}_rcpv", [128, 1], F32)
        rstd = p.sb(f"{tag}_rstd", [128, 1], F32)
        nmr = p.sb(f"{tag}_nmr", [128, 1], F32)

        sx = [p.sem(f"{tag}_sxA"), p.sem(f"{tag}_sxB")]
        sxo = [p.sem(f"{tag}_sxoA"), p.sem(f"{tag}_sxoB")]
        sv = p.sem(f"{tag}_sv")
        ss = p.sem(f"{tag}_ss")
        st = p.sem(f"{tag}_st")
        swv = p.sem(f"{tag}_swv")
        swq = [p.sem(f"{tag}_swqA"), p.sem(f"{tag}_swqB")]
        swk = [p.sem(f"{tag}_swkA"), p.sem(f"{tag}_swkB")]
        svt = p.sem(f"{tag}_svt")
        sqp = p.sem(f"{tag}_sqp")
        sex = p.sem(f"{tag}_sex")
        sbc = p.sem(f"{tag}_sbc")
        sao = [p.sem(f"{tag}_saoA"), p.sem(f"{tag}_saoB")]
        saov = p.sem(f"{tag}_saov")
        sai = p.sem(f"{tag}_sai")
        swo = p.sem(f"{tag}_swo")
        sop = p.sem(f"{tag}_sop")
        sdr = p.sem(f"{tag}_sdr")

        def vh(meth, *args, extra=None, **kw):
            inc = [(sv, 1)] + (extra or [])
            p.v.op(meth, *args, inc=inc, **kw)
            p.v.wait(sv, sv.n)

        p.g.op("dma_start", out=ropec_sb[:, :], in_=dropec, inc=[(sg0, 16)])
        p.g.op("dma_start", out=ropes_sb[:, :], in_=dropes, inc=[(sg0, 16)])
        p.v.wait(sg0, sg0.n)
        p.t.wait(sg0, sg0.n)

        def emit_ln(seq):
            base = seq * c.S
            for tt in range(c.ST):
                t0 = base + tt * 128
                slot = tt % 2
                p.guard_wait(p.y, ("ax", slot))
                p.y.op("dma_start", out=xr[slot][:, :], in_=src[t0:t0 + 128, :],
                       inc=[(sx[slot], 16)])
                xa = xr[slot]
                p.v.wait(sx[slot], sx[slot].n)
                vh("reduce_sum", mstat[:, :], xa[:, :], axis=AX.X)
                p.s.wait(sx[slot], sx[slot].n)
                p.s.op("activation", sq[:, :], xa[:, :], AF.Square, accum_out=ssq[:, :],
                       inc=[(ss, 1)])
                p.v.wait(ss, ss.n)
                vh("tensor_scalar", v1e[:, :], ssq[:, :], 1.0 / c.H, c.eps, ALU.mult, ALU.add)
                vh("tensor_scalar", mstat[:, :], mstat[:, :], 1.0 / c.H, None, ALU.mult)
                vh("tensor_mul", mm2[:, :], mstat[:, :], mstat[:, :])
                vh("tensor_sub", varv[:, :], v1e[:, :], mm2[:, :])
                vh("reciprocal", rcpv[:, :], varv[:, :])
                p.s.wait(sv, sv.n)
                p.s.op("activation", rstd[:, :], rcpv[:, :], AF.Sqrt, inc=[(ss, 1)])
                p.v.wait(ss, ss.n)
                vh("tensor_scalar", nmr[:, :], mstat[:, :], rstd[:, :], -1.0, ALU.mult, ALU.mult)
                p.guard_wait(p.v, ("azbf",))
                vh("tensor_scalar", zbf[slot][:, :], xa[:, :], rstd[:, :], nmr[:, :],
                   ALU.mult, ALU.add)
                p.guard_set(("ax", slot), (sv, sv.n), (ss, ss.n))
                p.t.wait(sv, sv.n)
                for k in range(c.KH):
                    dst_bf = pbank_bf(k % 2, 128, 0)
                    if k >= 2:
                        p.t.wait(sv, sv.n)
                    p.t.op("transpose", dst_bf, zbf[slot][:, k * 128:(k + 1) * 128],
                           identbf[:, :], inc=[(st, 1)])
                    p.v.wait(st, st.n)
                    p.v.op("tensor_copy", zTa[:, k, tt * 128:(tt + 1) * 128], dst_bf,
                           inc=[(sv, 1)])
                p.guard_set(("azbf",), (st, st.n))

        proj_ctr = [0]
        ao_ctr = [0]
        wsk_dummy = None

        def issue_slices(hd):
            qs = hd % 2
            p.g.wait(sqp, sqp.n - 2 * c.NQC)
            p.g.op("dma_start",
                   out=wsq[qs].ap().rearrange("p a b -> p (a b)"),
                   in_=dwq[hd], inc=[(swq[qs], 16)])
            p.g.op("dma_start",
                   out=wsk[qs].ap().rearrange("p a b -> p (a b)"),
                   in_=dwk[hd], inc=[(swk[qs], 16)])

        def emit_heads(seq):
            for g in range(c.NG):
                p.g.wait(svt, svt.n)
                p.g.wait(st, st.n)
                p.g.op("dma_start",
                       out=wvsl.ap().rearrange("p a b -> p (a b)"),
                       in_=dwv[g], inc=[(swv, 16)])
                p.t.wait(swv, swv.n)
                p.t.wait(sv, sv.n)
                p.t.wait(sqp, sqp.n)
                p.t.wait(sex, sex.n)
                for r in range(c.ST):
                    vp = pbank(r % 2, GW)
                    if r >= 2:
                        p.t.wait(svt, svt.n - 1)
                    for k in range(c.KH):
                        p.t.op("matmul", vp, zTa[:, k, r * 128:(r + 1) * 128],
                               wvsl[:, k, :], start=(k == 0), stop=False)
                    p.t.op("matmul", vp, ones_row_bf[:, :],
                           bvr_sb[0:1, g * GW:(g + 1) * GW],
                           start=False, stop=True, inc=[(st, 1)])
                    p.s.wait(st, st.n)
                    p.s.op("copy", vTg[:, r, :], vp, inc=[(svt, 1)])
                for hh in range(c.GH):
                    hd = g * c.GH + hh
                    qslot = hd % 2
                    if hd == 0:
                        issue_slices(0)
                    for which, wsl, wsem, reg, bias in (
                            (0, wsq, swq, qr, bqc_sb), (1, wsk, wsk_dummy, kr, bkc_sb)):
                        wsem = swq if which == 0 else swk
                        p.t.wait(wsem[qslot], wsem[qslot].n)
                        p.t.wait(svt, svt.n)
                        p.t.wait(sex, sex.n)
                        for qc in range(c.NQC):
                            pp = pbank(proj_ctr[0] % 2, c.QC)
                            proj_ctr[0] += 1
                            p.t.wait(sqp, sqp.n - 1)
                            for k in range(c.KH):
                                p.t.op("matmul", pp, wsl[qslot][:, k, :],
                                       zTa[:, k, qc * c.QC:(qc + 1) * c.QC],
                                       start=(k == 0), stop=(k == c.KH - 1),
                                       inc=[(st, 1)] if k == c.KH - 1 else None)
                            p.s.wait(st, st.n)
                            p.s.op("activation", reg[qslot][:, qc * c.QC:(qc + 1) * c.QC],
                                   pp, AF.Identity, bias=bias[:, hd:hd + 1],
                                   inc=[(sqp, 1)])
                        CH = min(512, c.S)
                        for ch in range(c.S // CH):
                            cols = slice(ch * CH, (ch + 1) * CH)
                            colabs = slice(seq * 0 + ch * CH, ch * CH + CH)
                            rp = pbank(6 + ch % 2, CH)
                            p.t.wait(sqp, sqp.n)
                            p.t.wait(sbc, sbc.n)
                            p.t.wait(sv, sv.n)
                            p.t.op("matmul", rp, rtbf[:, :], reg[qslot][:, cols],
                                   start=True, stop=True, inc=[(st, 1)])
                            p.v.wait(st, st.n)
                            vh("tensor_mul", tmp1[:, 0:CH], reg[qslot][:, cols],
                               ropec_sb[:, cols])
                            vh("tensor_tensor", tmp2[:, 0:CH], rp, ropes_sb[:, cols],
                               ALU.mult)
                            vh("tensor_add", reg[qslot][:, cols], tmp1[:, 0:CH],
                               tmp2[:, 0:CH])
                    if hd + 1 < c.NH:
                        issue_slices(hd + 1)
                    p.t.wait(sv, sv.n)
                    for qc in range(c.NQC):
                        aslot = ao_ctr[0] % 2
                        eslot = aslot
                        sump = pbank(6, c.QC)[0:1, :]
                        for kt in range(c.KT):
                            sp = pbank(2 + kt % 2, c.QC)
                            p.t.wait(sex, sex.n - 1)
                            p.t.op("matmul", sp, kr[qslot][:, kt * 128:(kt + 1) * 128],
                                   qr[qslot][:, qc * c.QC:(qc + 1) * c.QC],
                                   start=True, stop=True, inc=[(st, 1)])
                            p.s.wait(st, st.n)
                            p.s.op("activation", expT[eslot][:, kt, :], sp, AF.Exp,
                                   scale=float(1.0 / math.sqrt(128.0)),
                                   inc=[(sex, 1)])
                            p.t.wait(sex, sex.n)
                            if kt == 0:
                                p.t.wait(sv, sv.n)
                            p.t.op("matmul", sump, ones_col_bf[:, :],
                                   expT[eslot][:, kt, :],
                                   start=(kt == 0), stop=(kt == c.KT - 1),
                                   inc=[(st, 1)] if kt == c.KT - 1 else None)
                        avp = pbank(4 + aslot, c.QC)
                        p.t.wait(saov, saov.n - 1)
                        for kt in range(c.KT):
                            p.t.op("matmul", avp, vTg[:, kt, hh * 128:(hh + 1) * 128],
                                   expT[eslot][:, kt, :],
                                   start=(kt == 0), stop=(kt == c.KT - 1),
                                   inc=[(st, 1)] if kt == c.KT - 1 else None)
                        p.v.wait(st, st.n)
                        vh("tensor_copy", sums_sb[:, :], sump)
                        vh("reciprocal", rsum[:, :], sums_sb[:, :])
                        p.t.wait(sv, sv.n)
                        p.t.wait(sbc, sbc.n)
                        bcp = pbank(7, c.QC)
                        p.t.op("matmul", bcp, ones_row_f[:, :], rsum[:, :],
                               start=True, stop=True, inc=[(st, 1)])
                        p.s.wait(st, st.n)
                        p.s.op("copy", bc_sb[:, :], bcp, inc=[(sbc, 1)])
                        p.v.wait(sbc, sbc.n)
                        p.v.wait(sao[aslot], sao[aslot].n)
                        p.v.op("tensor_tensor", ao_sb[aslot][:, :], avp,
                               bc_sb[:, :], ALU.mult, inc=[(saov, 1), (sv, 1)])
                        p.v.wait(sv, sv.n)
                        p.y.wait(saov, saov.n)
                        p.y.op("dma_start",
                               out=dao[qc, :, hd * c.QC:(hd + 1) * c.QC],
                               in_=ao_sb[aslot][:, :], inc=[(sao[aslot], 16)])
                        ao_ctr[0] += 1

        def emit_oproj(seq):
            p.sb_reset("attn_oproj")
            aoin = [p.sb(f"{tag}_aoin{i}_{seq}", [128, c.KH, c.QC], BF16) for i in range(2)]
            wot_sb = p.sb(f"{tag}_wot_{seq}", [128, c.KH, c.H], BF16)
            o_sb = p.sb(f"{tag}_osb_{seq}", [128, c.IT, c.QC], F32)
            xo = [p.sb(f"{tag}_xo{i}_{seq}", [128, c.H], F32) for i in range(2)]
            base = seq * c.S
            p.g.op("dma_start",
                   out=wot_sb.ap().rearrange("p a b -> p (a b)"),
                   in_=dwo, inc=[(swo, 16)])
            gsz = min(4, c.IT)
            for tc in range(c.NQC):
                islot = tc % 2
                p.y.wait(sop, sop.n)
                p.y.op("dma_start",
                       out=aoin[islot].ap().rearrange("p a b -> p (a b)"),
                       in_=dao[tc], inc=[(sai, 16)])
                p.t.wait(sai, sai.n)
                p.t.wait(swo, swo.n)
                for it in range(c.IT):
                    op_ = pbank(it % 2, c.QC)
                    if it >= 2:
                        p.t.wait(sop, sop.n - 1)
                    for k in range(c.KH):
                        p.t.op("matmul", op_, wot_sb[:, k, it * 128:(it + 1) * 128],
                               aoin[islot][:, k, :],
                               start=(k == 0), stop=(k == c.KH - 1),
                               inc=[(st, 1)] if k == c.KH - 1 else None)
                    p.s.wait(st, st.n)
                    p.s.op("copy", o_sb[:, it, :], op_, inc=[(sop, 1)])
                for ttl in range(c.QC // 128):
                    t0 = base + tc * c.QC + ttl * 128
                    slot = ttl % 2
                    p.guard_wait(p.y, ("ax", slot))
                    p.y.op("dma_start", out=xr[slot][:, :], in_=src[t0:t0 + 128, :],
                           inc=[(sx[slot], 16)])
                    p.v.wait(sxo[slot], sxo[slot].n)
                    p.t.wait(sop, sop.n)
                    for grp in range(c.IT // gsz):
                        bk = 2 + (sdr.n % 2)
                        p.t.wait(sdr, sdr.n - 1)
                        for q in range(gsz):
                            it = grp * gsz + q
                            p.t.op("transpose", pbank(bk, 128, q * 128),
                                   o_sb[:, it, ttl * 128:(ttl + 1) * 128], ident32[:, :],
                                   inc=[(st, 1)])
                        p.v.wait(st, st.n)
                        p.v.wait(sx[slot], sx[slot].n)
                        span = gsz * 128
                        p.v.op("tensor_tensor", xo[slot][:, grp * span:(grp + 1) * span],
                               xr[slot][:, grp * span:(grp + 1) * span], pbank(bk, span),
                               ALU.add, inc=[(sdr, 1)])
                    p.guard_set(("ax", slot), (sdr, sdr.n))
                    p.y.wait(sdr, sdr.n)
                    p.y.op("dma_start", out=dst[t0:t0 + 128, :], in_=xo[slot][:, :],
                           inc=[(sxo[slot], 16)])

        for seq in range(c.Bc):
            emit_ln(seq)
            emit_heads(seq)
            p.y.wait(sao[0], sao[0].n)
            p.y.wait(sao[1], sao[1].n)
            p.barrier()
            emit_oproj(seq)
            p.y.wait(sxo[0], sxo[0].n)
            p.y.wait(sxo[1], sxo[1].n)
            p.barrier()

    def emit_attn_stub(tag, src, dst):
        scp = p.sem(f"{tag}_cp")
        buf = p.sb(f"{tag}_cpbuf", [128, c.H], F32)
        for t0 in range(0, c.N, 128):
            p.y.wait(scp, scp.n)
            p.y.op("dma_start", out=buf[:, :], in_=src[t0:t0 + 128, :], inc=[(scp, 16)])
            p.y.wait(scp, scp.n)
            p.y.op("dma_start", out=dst[t0:t0 + 128, :], in_=buf[:, :], inc=[(scp, 16)])
        p.barrier()

    cur = dx
    if c.do_ffn1:
        emit_ffn("f1", cur, dx1 if (c.do_attn or c.do_ffn2) else dout)
        cur = dx1 if (c.do_attn or c.do_ffn2) else dout
    if c.do_attn:
        ad = dx2 if c.do_ffn2 else dout
        emit_attn("at", cur, ad)
        cur = ad
    if c.do_ffn2:
        emit_ffn("f2", cur, dout)
        cur = dout
    if cur is not dout:
        emit_attn_stub("fin", cur, dout)

    for name, s in p.sems.items():
        if "sxo" in name or name.endswith("_cp"):
            p.y.wait(s, s.n)
    p.replay()
    return nc


def make_consts(cfg: Cfg):
    c = cfg
    ident = np.eye(128, dtype=np.float32)
    R = np.zeros((128, 128), np.float32)
    R[:64, 64:] = -np.eye(64)
    R[64:, :64] = np.eye(64)
    RT = np.ascontiguousarray(R.T)
    inv = 1.0 / (c.rope_theta ** (np.arange(0, 128, 2, dtype=np.float32) / 128))
    fr = np.outer(inv, np.arange(c.S, dtype=np.float32))
    cs = np.cos(fr).astype(np.float32)
    sn = np.sin(fr).astype(np.float32)
    ropec = np.ascontiguousarray(np.concatenate([cs, cs], 0))
    ropes = np.ascontiguousarray(np.concatenate([sn, sn], 0))
    iota32 = np.ascontiguousarray(np.tile(np.arange(32, dtype=np.float32), (128, 1)))
    iotam = np.ascontiguousarray(np.tile(np.arange(32, dtype=np.float32), (128, 4)))
    return dict(ident=ident, rt=RT, ropec=ropec, ropes=ropes, iota32=iota32, iotam=iotam)


def host_inputs(cfg, x_core, Wq, bq, Wk, bk, Wv, bv, Wo, gw, gup, dwn, sim):
    c = cfg
    consts = make_consts(cfg)
    sim32 = np.zeros((32, 32), np.float32)
    sim32[:sim.shape[0], :sim.shape[1]] = sim
    GW = c.GH * 128

    def qk_tile(W):
        t = W.T.reshape(c.KH, 128, c.NH, 128).transpose(2, 1, 0, 3)
        return np.ascontiguousarray(t.reshape(c.NH, 128, c.KH * 128))

    wv_t = Wv.T.reshape(c.KH, 128, c.NG, GW).transpose(2, 1, 0, 3)
    wo_t = Wo.T.reshape(c.KH, 128, c.H).transpose(1, 0, 2)
    gw_t = gw.T.reshape(c.KH, 128, c.E).transpose(1, 0, 2)
    wgu_t = gup.reshape(c.E, 2, c.FT, 128, c.KH, 128).transpose(0, 2, 5, 4, 1, 3)
    wd_t = dwn.reshape(c.E, c.NWD, c.WDC * 128, c.FT, 128).transpose(0, 1, 4, 3, 2)
    return {
        "x": np.ascontiguousarray(x_core.reshape(c.N, c.H)),
        "wqt": qk_tile(Wq),
        "wkt": qk_tile(Wk),
        "wvt": np.ascontiguousarray(wv_t.reshape(c.NG, 128, c.KH * GW)),
        "wot": np.ascontiguousarray(wo_t.reshape(128, c.KH * c.H)),
        "bqc": np.ascontiguousarray(bq.reshape(c.KH, 128).T),
        "bkc": np.ascontiguousarray(bk.reshape(c.KH, 128).T),
        "bvr": np.ascontiguousarray(bv.reshape(1, c.H)),
        "gwt": np.ascontiguousarray(gw_t),
        "wgut": np.ascontiguousarray(wgu_t.reshape(c.E * c.FT, 128, c.KH * 2 * 128)),
        "wdt": np.ascontiguousarray(wd_t.reshape(c.E * c.NWD, 128, c.FT * c.WDC * 128)),
        "sim": sim32,
        **consts,
    }


def kernel(L0_attn_hidden_states, L0_attn_Wq, L0_attn_bq, L0_attn_Wk, L0_attn_bk,
           L0_attn_Wv, L0_attn_bv, L0_attn_Wo, L0_ffn_gate_weight,
           L0_ffn_gate_up_proj, L0_ffn_down_proj, L0_ffn_similarity):
    cfg = Cfg()
    x = np.asarray(L0_attn_hidden_states, dtype=np.float32)
    B = x.shape[0]
    ncores = 8
    per = B // ncores
    nc = build_nc(cfg)
    shared = host_inputs(
        cfg, x[0:per],
        np.asarray(L0_attn_Wq, np.float32), np.asarray(L0_attn_bq, np.float32),
        np.asarray(L0_attn_Wk, np.float32), np.asarray(L0_attn_bk, np.float32),
        np.asarray(L0_attn_Wv, np.float32), np.asarray(L0_attn_bv, np.float32),
        np.asarray(L0_attn_Wo, np.float32),
        np.asarray(L0_ffn_gate_weight, np.float32),
        np.asarray(L0_ffn_gate_up_proj, np.float32),
        np.asarray(L0_ffn_down_proj, np.float32),
        np.asarray(L0_ffn_similarity, np.float32))
    in_maps = []
    for ci in range(ncores):
        m = dict(shared)
        m["x"] = np.ascontiguousarray(
            x[ci * per:(ci + 1) * per].reshape(cfg.N, cfg.H))
        in_maps.append(m)
    res = run_bass_kernel_spmd(nc, in_maps, core_ids=list(range(ncores)))
    outs = [r["out"].reshape(per, cfg.S, cfg.H) for r in res.results]
    return np.concatenate(outs, axis=0)
